# version 17
# baseline (speedup 1.0000x reference)
import contextlib
import numpy as np
import ml_dtypes
import concourse.bass as bass
import concourse.mybir as mybir
from concourse.bass_utils import run_bass_kernel_spmd

F32 = mybir.dt.float32
BF16 = mybir.dt.bfloat16
FP16 = mybir.dt.float16
U8 = mybir.dt.uint8
AF = mybir.ActivationFunctionType
ALU = mybir.AluOpType
AX = mybir.AxisListType

T = 8192
D = 1024
H = 16
DK = 64
LORA = 64
FF = 4096
E = 8
HID = 2048
TB = 512
NTB = T // TB
NFB = D // 128
C = 32
NCH = T // C

EPS_LN = 1e-5
EPS_RMS = 1e-6
EPS_GN = 1e-5
CLAMP = 42.0


class Task:
    __slots__ = ("tid", "eng", "fn", "inc", "deps", "cum", "drain", "hard")

    def __init__(self, tid, eng, fn, inc, deps):
        self.tid, self.eng, self.fn, self.inc, self.deps = tid, eng, fn, inc, deps
        self.cum = 0
        self.drain = False
        self.hard = ()


class Sched:

    ENGS = ("t", "v", "a", "g", "q")

    def __init__(self):
        self.tasks = []
        self.per_eng = {e: [] for e in self.ENGS}

    def add(self, eng, fn, deps=(), dma=0, drain=False, hard=()):
        deps = [d for d in deps if d is not None]
        inc = 16 * dma if dma else 1
        t = Task(len(self.tasks), eng, fn, inc, deps)
        t.drain = drain
        t.hard = [d for d in hard if d is not None]
        for d in deps:
            assert d.tid < t.tid, "dep must precede task"
        self.tasks.append(t)
        self.per_eng[eng].append(t)
        t.cum = (self.per_eng[eng][-2].cum if len(self.per_eng[eng]) > 1 else 0) + inc
        return t

    def lasts(self):
        return [self.per_eng[e][-1] for e in self.ENGS if self.per_eng[e]]

    def emit(self, nc, block, sems):
        eng_objs = {"t": nc.tensor, "v": nc.vector, "a": nc.scalar,
                    "g": nc.gpsimd, "q": nc.sync}
        finals = {e: (self.per_eng[e][-1].cum if self.per_eng[e] else 0)
                  for e in self.ENGS}

        def mk(eng):
            def emit_stream(inner):
                watermark = {e: 0 for e in self.ENGS}
                for t in self.per_eng[eng]:
                    need = {}
                    for d in t.deps:
                        if d.eng == eng:
                            continue
                        need[d.eng] = max(need.get(d.eng, 0), d.cum)
                    for d in t.hard:
                        need[d.eng] = max(need.get(d.eng, 0), d.cum)
                    for pe in self.ENGS:
                        if pe in need and need[pe] > watermark[pe]:
                            inner.wait_ge(sems[pe], need[pe])
                            watermark[pe] = need[pe]
                    r = t.fn(inner)
                    if t.drain:
                        inner.drain()
                        r = None if r is None else r
                        inner.nop().then_inc(sems[eng], t.inc)
                        continue
                    if t.inc >= 16:
                        assert isinstance(r, (list, tuple)) and \
                            len(r) * 16 == t.inc, f"dma task {t.tid} count"
                        for inst in r:
                            inst.then_inc(sems[eng], 16)
                    elif r is None:
                        inner.nop().then_inc(sems[eng], 1)
                    else:
                        r.then_inc(sems[eng], 1)
                if eng == "q":
                    for pe in self.ENGS:
                        if pe != "q" and finals[pe] > watermark[pe]:
                            inner.wait_ge(sems[pe], finals[pe])
                    inner.wait_ge(sems["q"], finals["q"])
            return emit_stream

        block.tensor(mk("t"))
        block.vector(mk("v"))
        block.scalar(mk("a"))
        block.gpsimd(mk("g"))
        block.sync(mk("q"))


class Res:

    def __init__(self, name=""):
        self.name, self.writer, self.readers = name, None, []

    def rdeps(self):
        return [self.writer] if self.writer else []

    def wdeps(self):
        return ([self.writer] if self.writer else []) + self.readers

    def did_read(self, task):
        self.readers.append(task)
        return task

    def did_write(self, task):
        self.writer, self.readers = task, []
        return task


def build_program(debug_taps=(), dbg_raw_o=False, skip=()):
    nc = bass.Bass("TRN2", target_bir_lowering=False, debug=False)

    def inp(name, shape, dtype=F32):
        return nc.dram_tensor(name, list(shape), dtype, kind="ExternalInput").ap()

    x_in = inp("x", (T, D))
    Wr = inp("Wr", (D, D)); Wk = inp("Wk", (D, D)); Wv = inp("Wv", (D, D))
    Wg = inp("Wg", (D, D)); Wo = inp("Wo", (D, D))
    w0 = inp("w0", (D,))
    wA = inp("wA", (D, LORA)); wB = inp("wB", (LORA, D))
    ffn_w1 = inp("ffn_w1", (D, FF)); ffn_w2 = inp("ffn_w2", (FF, D))
    router_w = inp("router_w", (D, E))
    gating_w = inp("gating_w", (E, 2, HID, D))
    linear_w = inp("linear_w", (E, HID, D))
    vec_cols = inp("vec_cols", (128, 8 * NFB))
    m_low_i = inp("m_low", (128, 512), U8)
    m_hi_i = inp("m_hi", (128, 512), U8)
    m_D1_i = inp("m_D1", (128, 512), U8)
    m_D2_i = inp("m_D2", (128, 512), U8)
    gnw_rep = inp("gnw_rep", (128, D))
    gnb_rep = inp("gnb_rep", (128, D))
    ln2w_rep = inp("ln2w_rep", (128, D))
    pes_rep = inp("pes_rep", (128, E))
    ident_in = inp("ident", (128, 128))

    out_t = nc.dram_tensor("out", [T, D], F32, kind="ExternalOutput").ap()

    def plane(name, shape, dtype=BF16):
        return nc.dram_tensor(name, list(shape), dtype, kind="Internal").ap()

    P_xn = plane("P_xn", (T, D), FP16)
    P_u = plane("P_u", (T, D), FP16)
    P_v = plane("P_v", (T, D), FP16)
    P_g = plane("P_g", (T, D), FP16)
    P_rt16 = plane("P_rt16", (D, T))
    P_kt16 = plane("P_kt16", (D, T))
    P_rc16 = plane("P_rc16", (D, T), FP16)
    P_kc16 = plane("P_kc16", (D, T), FP16)
    P_ro32 = plane("P_ro32", (D, T), FP16)
    P_ks32 = plane("P_ks32", (D, T), FP16)
    P_ku = plane("P_ku", (D, T), FP16)
    P_rr = plane("P_rr", (D, T), FP16)
    P_dec = plane("P_dec", (D, NCH), F32)
    P_ct = plane("P_ct", (T, E), F32)
    P_o = plane("P_o", (T, D), FP16)
    P_xa = plane("P_xa", (T, D), F32)
    P_h2 = plane("P_h2", (T, D), FP16)
    P_s = plane("P_s", (T, D), F32)
    P_gw16 = plane("P_gw16", (E, 2, HID, D), FP16)
    P_gwT = plane("P_gwT", (E, 2, D, HID), FP16)
    P_m = plane("P_m", (T, D), FP16)
    P_odbg = plane("P_odbg", (T, D), F32) if dbg_raw_o else None
    P_adbg = plane("P_adbg", (HID, 512), F32) if dbg_raw_o else None

    planes = dict(P_xn=P_xn, P_u=P_u, P_v=P_v, P_g=P_g, P_rt16=P_rt16,
                  P_kt16=P_kt16, P_rc16=P_rc16, P_kc16=P_kc16, P_ro32=P_ro32,
                  P_ks32=P_ks32, P_ku=P_ku, P_rr=P_rr, P_dec=P_dec, P_ct=P_ct,
                  P_o=P_o, P_xa=P_xa, P_h2=P_h2, P_s=P_s, P_m=P_m,
                  P_gwT=P_gwT)
    if dbg_raw_o:
        planes["P_odbg"] = P_odbg
        planes["P_adbg"] = P_adbg
    taps = {n: nc.dram_tensor("tap_" + n, list(planes[n].shape),
                              planes[n].dtype, kind="ExternalOutput").ap()
            for n in debug_taps}

    sched = Sched()
    stack = contextlib.ExitStack()
    cur = [None]

    def sbuf(name, shape, dtype=F32):
        st = cur[0] if cur[0] is not None else stack
        return st.enter_context(nc.sbuf_tensor(name, list(shape), dtype))

    def phase_barrier():
        for e in Sched.ENGS:
            sched.add(e, lambda x: None,
                      deps=[t for t in sched.lasts()])

    with stack:
        sems = {e: stack.enter_context(nc.semaphore("sem_" + e))
                for e in Sched.ENGS}
        PS = stack.enter_context(nc.psum_tensor("PS", [128, 4096], F32))

        def bank(i, cols=512):
            return PS[:, i * 512:i * 512 + cols]

        cvec = sbuf("cvec", (128, 8 * NFB), F32)
        c_pes = sbuf("c_pes", (128, E), F32)
        c_ones = sbuf("c_ones", (128, 512), BF16)
        c_eps = sbuf("c_eps", (128, 4), F32)

        def vcol(vi, fb):
            return cvec[:, vi * NFB + fb:vi * NFB + fb + 1]
        V_LN1, V_MUR, V_MUK, V_MUV, V_MUW, V_MUG, V_U, V_RSC = range(8)

        t_const = sched.add("q", lambda e: [
            e.dma_start(out=cvec[:], in_=vec_cols),
            e.dma_start(out=c_pes[:], in_=pes_rep),
        ], dma=2)
        def f_memc(e):
            e.memset(c_ones[:], 1.0)
            e.memset(c_eps[:, 0:1], EPS_LN)
            e.memset(c_eps[:, 1:2], EPS_RMS)
            return e.memset(c_eps[:, 2:3], EPS_GN)
        t_memc = sched.add("v", f_memc)

        ph01 = contextlib.ExitStack()
        cur[0] = ph01
        t_gwc = sched.add("g", lambda e: [
            e.dma_start(out=P_gw16[e_, g_], in_=gating_w[e_, g_])
            for e_ in range(E) for g_ in range(2)], dma=16)
        gtt = [sbuf(f"gtt{i}", (128, HID), FP16) for i in range(2)]
        gtt_res = [Res(f"gtt{i}") for i in range(2)]
        gti = 0
        for e_ in range(E):
            for g_ in range(2):
                for dc in range(NFB):
                    gb_ = gti % 2
                    gti += 1
                    t_t1 = sched.add("q", (lambda e, e_=e_, g_=g_, dc=dc,
                                           gb_=gb_: [
                        e.dma_start(out=gtt[gb_][:],
                                    in_=P_gw16[e_, g_][:, dc * 128:(dc + 1) * 128],
                                    transpose=True)]),
                        deps=[t_gwc] + gtt_res[gb_].wdeps(), dma=1)
                    gtt_res[gb_].did_write(t_t1)
                    t_t2 = sched.add("g", (lambda e, e_=e_, g_=g_, dc=dc,
                                           gb_=gb_: [
                        e.dma_start(out=P_gwT[e_, g_, dc * 128:(dc + 1) * 128, :],
                                    in_=gtt[gb_][:])]), deps=[t_t1], dma=1)
                    gtt_res[gb_].did_read(t_t2)


        _p0x = sbuf("p0_x", (128, D), F32)
        _p0xn = sbuf("p0_xn", (128, D), F32)
        p0_x = [_p0x, _p0x]
        p0_xn = [_p0xn, _p0xn]
        p0_xnb = [sbuf(f"p0_xnb{i}", (128, D), FP16) for i in range(2)]
        p0_ub = [sbuf(f"p0_ub{i}", (128, D), FP16) for i in range(2)]
        p0_stat = [sbuf(f"p0_st{i}", (128, 8), F32) for i in range(2)]
        _p0xr = Res()
        p0_x_res = [_p0xr, _p0xr]
        p0_o_res = [Res() for _ in range(2)]
        p0_s_res = [Res() for _ in range(2)]
        sq_scr_t = sbuf("sq_scr", (128, D), F32)
        sq_scr = sq_scr_t[:]

        p0_done = []
        for i in range(T // 128):
            b = i % 2
            xb, xnb, xnbb, ub, st = (p0_x[b], p0_xn[b], p0_xnb[b], p0_ub[b],
                                     p0_stat[b])
            xr_, or_, sr_ = p0_x_res[b], p0_o_res[b], p0_s_res[b]

            t_ld = sched.add("q", (lambda e, d=xb, s=x_in[i * 128:(i + 1) * 128, :]:
                                   [e.dma_start(out=d[:], in_=s)]),
                             deps=xr_.wdeps(), dma=1)
            xr_.did_write(t_ld)

            def f_mean(e, xb=xb, st=st):
                e.reduce_sum(st[:, 0:1], xb[:], axis=AX.X)
                e.drain()
                return e.tensor_scalar(st[:, 0:1], st[:, 0:1], 1.0 / D, None,
                                       ALU.mult)
            t_mean = sched.add("v", f_mean, drain=True,
                               deps=xr_.rdeps() + [t_memc] + sr_.wdeps())
            xr_.did_read(t_mean)
            sr_.did_write(t_mean)

            def f_sq(e, xb=xb, st=st):
                return e.activation(sq_scr, xb[:], AF.Square,
                                    accum_out=st[:, 2:3])
            t_sq = sched.add("a", f_sq, deps=xr_.rdeps() + [t_mean])
            xr_.did_read(t_sq)

            def f_var(e, st=st):
                e.tensor_scalar(st[:, 3:4], st[:, 2:3], 1.0 / D, None, ALU.mult)
                e.tensor_tensor(st[:, 4:5], st[:, 0:1], st[:, 0:1], ALU.mult)
                e.drain()
                return e.tensor_tensor(st[:, 3:4], st[:, 3:4], st[:, 4:5],
                                       ALU.subtract)
            t_var = sched.add("v", f_var, deps=[t_sq], drain=True)

            def f_rstd(e, st=st):
                e.activation(st[:, 1:2], st[:, 3:4], AF.Ln, bias=c_eps[:, 0:1])
                e.drain()
                return e.activation(st[:, 1:2], st[:, 1:2], AF.Exp, scale=-0.5)
            t_rstd = sched.add("a", f_rstd, deps=[t_var, t_memc], drain=True)

            def f_xn(e, xb=xb, xnb=xnb, st=st):
                return e.tensor_scalar(xnb[:], xb[:], st[:, 0:1], st[:, 1:2],
                                       ALU.subtract, ALU.mult)
            t_xn = sched.add("v", f_xn, deps=xr_.rdeps() + [t_rstd])
            xr_.did_read(t_xn)

            def f_ss2(e, xnb=xnb, st=st):
                return e.activation(sq_scr, xnb[:], AF.Square,
                                    accum_out=st[:, 5:6])
            t_ss2 = sched.add("a", f_ss2, deps=[t_xn])

            def f_s1(e, st=st):
                e.activation(st[:, 6:7], st[:, 5:6], AF.Ln, bias=c_eps[:, 1:2],
                             scale=1.0 / D)
                e.drain()
                return e.activation(st[:, 6:7], st[:, 6:7], AF.Exp, scale=-0.5)
            t_s1 = sched.add("a", f_s1, deps=[t_ss2, t_memc], drain=True)

            def f_fin(e, xnb=xnb, xnbb=xnbb, ub=ub, st=st):
                e.tensor_copy(xnbb[:], xnb[:])
                return e.tensor_scalar(ub[:], xnb[:], st[:, 6:7], None, ALU.mult)
            t_fin = sched.add("v", f_fin, deps=[t_s1] + or_.wdeps())
            or_.did_write(t_fin)
            sr_.did_read(t_fin)

            t_sp = sched.add("g", (lambda e, xnbb=xnbb, ub=ub, i=i: [
                e.dma_start(out=P_xn[i * 128:(i + 1) * 128, :], in_=xnbb[:]),
                e.dma_start(out=P_u[i * 128:(i + 1) * 128, :], in_=ub[:]),
            ]), deps=[t_fin], dma=2)
            or_.did_read(t_sp)
            p0_done.append(t_sp)

        bres = [Res(f"bank{i}") for i in range(8)]

        wr_s = sbuf("wr_s", (128, NFB, D), FP16)
        wk_s = sbuf("wk_s", (128, NFB, D), FP16)
        wv_s = sbuf("wv_s", (128, NFB, D), FP16)
        wg_s = sbuf("wg_s", (128, NFB, D), FP16)
        wa_s = sbuf("wa_s", (128, NFB, LORA), FP16)
        wbx_s = sbuf("wbx_s", (65, D), FP16)
        rw_s = sbuf("rw_s", (128, NFB, E), FP16)
        t_wld = sched.add("g", lambda e: [
            e.dma_start(out=wr_s[:], in_=Wr.rearrange("(b p) o -> p b o", p=128)),
            e.dma_start(out=wk_s[:], in_=Wk.rearrange("(b p) o -> p b o", p=128)),
            e.dma_start(out=wv_s[:], in_=Wv.rearrange("(b p) o -> p b o", p=128)),
            e.dma_start(out=wg_s[:], in_=Wg.rearrange("(b p) o -> p b o", p=128)),
            e.dma_start(out=wa_s[:], in_=wA.rearrange("(b p) o -> p b o", p=128)),
            e.dma_start(out=wbx_s[0:64, :], in_=wB),
            e.dma_start(out=wbx_s[64:65, :], in_=w0.unsqueeze(0)),
            e.dma_start(out=rw_s[:], in_=router_w.rearrange("(b p) o -> p b o", p=128)),
        ], dma=8)

        ufm = [sbuf(f"ufm{i}", (128, NFB, 512), FP16) for i in range(2)]
        ufm_res = [Res(f"ufm{i}") for i in range(2)]
        ucar = sbuf("ucar", (128, NFB), F32)
        ucar_res = Res("ucar")
        h_s = sbuf("h_s", (128, NFB, 513), FP16)
        h_res = Res("h")
        xx_s = sbuf("xx_s", (128, NFB, 512), FP16)
        xx_res = Res("xx")
        slab = [sbuf(f"xslab{i}", (128, NFB, 512), FP16) for i in range(2)]
        slab_res = [Res(f"slab{i}") for i in range(2)]

        e_s = sbuf("e_s", (128, 512), F32)
        es_res = Res("e_s")
        Ls = sbuf("Ls", (128, NFB, 513), F32)
        Ls_res = Res("Ls")
        t_ls0 = sched.add("v", lambda e: e.memset(Ls[:, :, 0:1], 0.0),
                          drain=True)
        Ls_res.did_write(t_ls0)
        th_s = sbuf("th_s", (65, 512), BF16)
        th_res = Res("th")
        argt = [sbuf(f"argt{i}", (128, 512), F32) for i in range(2)]
        arg_res = [Res(f"arg{i}") for i in range(2)]
        exf_t = sbuf("exf", (128, 512), BF16)
        exf = [exf_t, exf_t]
        exf_r1 = Res("exf")
        exf_res = [exf_r1, exf_r1]
        m16_t = sbuf("m16_t", (128, 32), F32)
        m16_res = Res("m16")
        dec_t = sbuf("dec_t", (128, 16), F32)
        decx = sbuf("decx", (128, 16), F32)
        dec_res = Res("dec")
        FPL = ("rt16", "rc16", "rr", "kt16", "kc16", "ks32", "ku", "ro32")
        kk_stg = sbuf("kk_stg", (128, 512), FP16)
        kk_res = Res("kk_stg")
        fstg = {n: sbuf("stg_" + n, (128, 512),
                        BF16 if n in ("rt16", "kt16") else FP16) for n in FPL}
        fstg_res = {n: Res("stg_" + n) for n in FPL}
        fplane = {"rt16": P_rt16, "rc16": P_rc16, "rr": P_rr, "kt16": P_kt16,
                  "kc16": P_kc16, "ks32": P_ks32, "ku": P_ku, "ro32": P_ro32}
        vsp = sbuf("vsp", (128, 4, D), FP16)
        vsp_res = Res("vsp")
        gsp = sbuf("gsp", (128, 4, D), FP16)
        gsp_res = Res("gsp")
        lgs = sbuf("lgs", (8, 512), F32)
        lgs_res = Res("lgs")
        ltm = sbuf("ltm", (128, 4, E), F32)
        rt_a = sbuf("rt_a", (128, 4, E), F32)
        rt_b = sbuf("rt_b", (128, 4, E), F32)
        rt_c = sbuf("rt_c", (128, 4, E), F32)
        rt_u = sbuf("rt_u", (128, 4, E), U8)
        rt_st = sbuf("rt_st", (128, 16), F32)
        rtr_res = Res("router")
        cfm_s = sbuf("cfm_s", (8, 512), F32)
        cfm_res = Res("cfm")
        router_prev = [None]
        c_ident = sbuf("c_ident", (128, 128), F32)
        c_neginf = sbuf("c_neginf", (128, E), F32)
        t_ident = sched.add("q", lambda e: [
            e.dma_start(out=c_ident[:], in_=ident_in)], dma=1)
        t_ninf = sched.add("v", lambda e: e.memset(c_neginf[:], -1e30))
        t_thones = sched.add("v", lambda e: e.memset(th_s[64:65, :], 1.0))
        th_res.did_write(t_thones)

        def bcast8(t2d):
            return t2d.unsqueeze(2).broadcast_to([128, 4, E])

        p1_last = []
        for tb in range(NTB):
            toff = 0
            buf = tb % 2
            uf, ures = ufm[buf], ufm_res[buf]
            if tb > 0:
                pu, pres = ufm[(tb - 1) % 2], ufm_res[(tb - 1) % 2]

                def f_ucar(e, pu=pu):
                    last = None
                    for fb in range(NFB):
                        last = e.tensor_scalar(
                            ucar[:, fb:fb + 1], pu[:, fb, 511:512],
                            vcol(V_LN1, fb), None, ALU.mult)
                    return last
                t_uc = sched.add("v", f_ucar,
                                 deps=pres.rdeps() + [t_const]
                                 + ucar_res.wdeps(), drain=True)
                pres.did_read(t_uc)
                ucar_res.did_write(t_uc)
            t_ul = sched.add("q", (lambda e, uf=uf, tb=tb: [
                e.dma_start(
                    out=uf[:, fb, :],
                    in_=P_u[tb * 512:(tb + 1) * 512,
                            fb * 128:(fb + 1) * 128],
                    transpose=True)
                for fb in range(NFB)]),
                deps=ures.wdeps() + p0_done[tb * 4:tb * 4 + 4],
                dma=NFB)
            ures.did_write(t_ul)

            def f_h(e, uf=uf, tb=tb):
                last = None
                for fb in range(NFB):
                    last = e.tensor_scalar(
                        h_s[:, fb, 1:513], uf[:, fb, 0:512],
                        vcol(V_LN1, fb), None, ALU.mult)
                if tb == 0:
                    last = e.memset(h_s[:, :, 0:1], 0.0)
                else:
                    last = e.tensor_copy(h_s[:, :, 0:1],
                                         ucar[:].unsqueeze(2))
                return last
            hdeps = ures.rdeps() + [t_const] + h_res.wdeps()
            if tb > 0:
                hdeps += ucar_res.rdeps()
            t_h = sched.add("v", f_h, deps=hdeps, drain=True)
            ures.did_read(t_h)
            h_res.did_write(t_h)
            if tb > 0:
                ucar_res.did_read(t_h)

            def f_xx(e):
                return e.tensor_tensor(xx_s[:], h_s[:, :, 0:512],
                                       h_s[:, :, 1:513], ALU.subtract)
            t_xx = sched.add("v", f_xx, deps=[t_h] + xx_res.wdeps())
            h_res.did_read(t_xx)
            xx_res.did_write(t_xx)

            def mk_slab(vi, sbuf_i):
                sl, sres = slab[sbuf_i], slab_res[sbuf_i]

                def f_sl(e):
                    last = None
                    for fb in range(NFB):
                        last = e.scalar_tensor_tensor(
                            sl[:, fb, :], xx_s[:, fb, :], vcol(vi, fb),
                            h_s[:, fb, 1:513], ALU.mult, ALU.add)
                    return last
                t_sl = sched.add("v", f_sl,
                                 deps=[t_xx] + sres.wdeps() + h_res.rdeps()[:0])
                h_res.did_read(t_sl)
                xx_res.did_read(t_sl)
                sres.did_write(t_sl)
                return t_sl

            t_xw = mk_slab(V_MUW, 0)
            pw = PS[0:64, 0:512]

            def f_wa(e, sl=slab[0]):
                last = None
                for dinb in range(NFB):
                    last = e.matmul(pw, wa_s[:, dinb, :], sl[:, dinb, :],
                                    start=(dinb == 0), stop=(dinb == NFB - 1))
                return last
            t_wa = sched.add("t", f_wa,
                             deps=[t_xw, t_wld] + bres[0].wdeps())
            slab_res[0].did_read(t_wa)
            bres[0].did_write(t_wa)

            def f_tanh(e):
                return e.activation(th_s[0:64, :], pw, AF.Tanh)
            t_tanh = sched.add("a", f_tanh, deps=[t_wa] + th_res.wdeps()
                               + [t_thones])
            bres[0].did_read(t_tanh)
            th_res.did_write(t_tanh)
            scans = []
            for b in range(NFB):
                bk = 1 + (b % 2)

                def f_wb(e, b=b, bk=bk):
                    return e.matmul(bank(bk), wbx_s[:, b * 128:(b + 1) * 128],
                                    th_s[:], start=True, stop=True)
                t_wb = sched.add("t", f_wb, deps=[t_tanh] + bres[bk].wdeps())
                th_res.did_read(t_wb)
                bres[bk].did_write(t_wb)

                def f_e(e, bk=bk):
                    return e.activation(e_s[:], bank(bk), AF.Exp)
                t_e = sched.add("a", f_e, deps=[t_wb] + es_res.wdeps())
                bres[bk].did_read(t_e)
                es_res.did_write(t_e)

                def f_scan(e, b=b, tb=tb):
                    r = e.tensor_tensor_scan(
                        Ls[:, b, 1:513], c_ones[:, 0:512], e_s[:],
                        Ls[:, b, 0:1] if tb > 0 else 0.0,
                        ALU.mult, ALU.subtract)
                    return r
                t_sc = sched.add("v", f_scan,
                                 deps=[t_e, t_memc] + Ls_res.wdeps(),
                                 drain=True)
                es_res.did_read(t_sc)
                Ls_res.did_write(t_sc)
                scans.append(t_sc)
                Lb = Ls[:, b, :]

                def f_dec(e, Lb=Lb):
                    e.tensor_tensor(dec_t[:], Lb[:, 32:513:32],
                                    Lb[:, 0:481:32], ALU.subtract)
                    e.drain()
                    return None
                t_dec = sched.add("v", f_dec, deps=[t_sc] + dec_res.wdeps(),
                                  drain=True)
                Ls_res.did_read(t_dec)

                def f_decx(e):
                    return e.activation(decx[:], dec_t[:], AF.Exp)
                t_dx = sched.add("a", f_decx, deps=[t_dec])
                dec_res.did_write(t_dx)
                t_dsp = sched.add("g", (lambda e, b=b, tb=tb: [
                    e.dma_start(out=P_dec[b * 128:(b + 1) * 128,
                                          tb * 16:(tb + 1) * 16],
                                in_=decx[:])]), deps=[t_dx], dma=1)
                dec_res.did_read(t_dsp)

            def factor(name, b, t_mm, bk, kind, clamp, extra_dep=None,
                       read_res=None):
                ai = 0 if name in ("rt16", "kt16", "ks32") else 1
                at, ar = argt[ai], arg_res[ai]
                xt, xr2 = exf[ai], exf_res[ai]
                Lb = Ls[:, b, :]
                Lx = Lb[:, 0:512].rearrange("p (c t) -> p c t", c=kind[3])
                Lv = Lb[:, 1:513].rearrange("p (c t) -> p c t", c=kind[3])

                def f_arg(e):
                    in0 = kind[0](Lb)
                    base = Lx if kind[2] == "lx" else Lv
                    r = e.scalar_tensor_tensor(
                        at[:].rearrange("p (c t) -> p c t", c=kind[3]),
                        in0, kind[1], base, ALU.mult,
                        ALU.add if kind[2] == "lx" else ALU.subtract)
                    if clamp:
                        r = e.tensor_scalar(at[:], at[:], CLAMP, None, ALU.min)
                    return r
                deps = [scans[b]] + ar.wdeps()
                if extra_dep is not None:
                    deps.append(extra_dep)
                t_a = sched.add("v", f_arg, deps=deps)
                Ls_res.did_read(t_a)
                if read_res is not None:
                    read_res.did_read(t_a)
                ar.did_write(t_a)

                def f_exp(e):
                    return e.activation(xt[:], at[:], AF.Exp)
                t_x = sched.add("a", f_exp, deps=[t_a] + xr2.wdeps())
                ar.did_read(t_x)
                xr2.did_write(t_x)

                stg, sres = fstg[name], fstg_res[name]
                src = factor.src

                def f_mul(e, src=src):
                    if src is None:
                        return e.tensor_tensor(stg[:], bank(bk), xt[:], ALU.mult)
                    return e.tensor_tensor(stg[:], src[:], xt[:], ALU.mult)
                t_m = sched.add("v", f_mul,
                                deps=[t_x, t_mm] + sres.wdeps()
                                + ([factor.src_task] if src is not None else []))
                xr2.did_read(t_m)
                if src is None:
                    bres[bk].did_read(t_m)
                else:
                    factor.src_res.did_read(t_m)
                sres.did_write(t_m)
                t_s = sched.add("g", (lambda e, name=name, b=b, tb=tb: [
                    e.dma_start(out=fplane[name][b * 128:(b + 1) * 128,
                                                 tb * 512:(tb + 1) * 512],
                                in_=stg[:])]), deps=[t_m], dma=1)
                sres.did_read(t_s)
                return t_s
            factor.src = None
            factor.src_task = None
            factor.src_res = None

            def g_end16(Lb):
                return Lb[:, 16:513:16].unsqueeze(2).broadcast_to([128, 32, 16])

            def g_start16(Lb):
                return Lb[:, 0:497:16].unsqueeze(2).broadcast_to([128, 32, 16])

            def g_end32(Lb):
                return Lb[:, 32:513:32].unsqueeze(2).broadcast_to([128, 16, 32])

            def g_start32(Lb):
                return Lb[:, 0:481:32].unsqueeze(2).broadcast_to([128, 16, 32])

            def g_m16(Lb):
                return m16_t[:].unsqueeze(2).broadcast_to([128, 32, 16])

            t_xr = mk_slab(V_MUR, 1)
            for b in range(NFB):
                bk = 3 + (b % 2)

                def f_mm(e, b=b, bk=bk, sl=slab[1]):
                    last = None
                    for dinb in range(NFB):
                        last = e.matmul(bank(bk),
                                        wr_s[:, dinb, b * 128:(b + 1) * 128],
                                        sl[:, dinb, :],
                                        start=(dinb == 0), stop=(dinb == NFB - 1))
                    return last
                t_mm = sched.add("t", f_mm, deps=[t_xr] + bres[bk].wdeps())
                slab_res[1].did_read(t_mm)
                bres[bk].did_write(t_mm)

                def f_m16(e, b=b):
                    e.tensor_tensor(m16_t[:], Ls[:, b, 16:513:16],
                                    Ls[:, b, 0:497:16], ALU.add)
                    e.drain()
                    return None
                t_m16 = sched.add("v", f_m16,
                                  deps=[scans[b]] + m16_res.wdeps(),
                                  drain=True)
                Ls_res.did_read(t_m16)
                m16_res.did_write(t_m16)

                stg, sres = fstg["rr"], fstg_res["rr"]

                def f_rr(e, bk=bk, stg=stg):
                    return e.tensor_copy(stg[:], bank(bk))
                t_rr = sched.add("v", f_rr, deps=[t_mm] + sres.wdeps())
                bres[bk].did_read(t_rr)
                sres.did_write(t_rr)
                factor.src, factor.src_task, factor.src_res = stg, t_rr, sres
                factor("rt16", b, t_mm, bk, (g_m16, -0.5, "lx", 32), True,
                       extra_dep=t_m16, read_res=m16_res)
                factor("rc16", b, t_mm, bk, (g_start16, -1.0, "lx", 32), False)
                factor("ro32", b, t_mm, bk, (g_start32, -1.0, "lx", 16), False)
                factor.src = None
                t_rrs = sched.add("g", (lambda e, b=b, tb=tb, stg=stg: [
                    e.dma_start(out=P_rr[b * 128:(b + 1) * 128,
                                         tb * 512:(tb + 1) * 512],
                                in_=stg[:])]), deps=[t_rr], dma=1)
                sres.did_read(t_rrs)

            t_xk = mk_slab(V_MUK, 0)
            for b in range(NFB):
                bk = 3 + (b % 2)

                def f_mm(e, b=b, bk=bk, sl=slab[0]):
                    last = None
                    for dinb in range(NFB):
                        last = e.matmul(bank(bk),
                                        wk_s[:, dinb, b * 128:(b + 1) * 128],
                                        sl[:, dinb, :],
                                        start=(dinb == 0), stop=(dinb == NFB - 1))
                    return last
                t_mm = sched.add("t", f_mm, deps=[t_xk] + bres[bk].wdeps())
                slab_res[0].did_read(t_mm)
                bres[bk].did_write(t_mm)

                def f_m16(e, b=b):
                    e.tensor_tensor(m16_t[:], Ls[:, b, 16:513:16],
                                    Ls[:, b, 0:497:16], ALU.add)
                    e.drain()
                    return None
                t_m16 = sched.add("v", f_m16,
                                  deps=[scans[b]] + m16_res.wdeps(),
                                  drain=True)
                Ls_res.did_read(t_m16)
                m16_res.did_write(t_m16)

                kstg, ksres = kk_stg, kk_res
                def f_kk(e, bk=bk):
                    return e.tensor_copy(kstg[:], bank(bk))
                t_kk = sched.add("v", f_kk, deps=[t_mm] + ksres.wdeps())
                bres[bk].did_read(t_kk)
                ksres.did_write(t_kk)
                factor.src, factor.src_task, factor.src_res = kstg, t_kk, ksres
                factor("kt16", b, t_mm, bk, (g_m16, 0.5, "lv", 32), True,
                       extra_dep=t_m16, read_res=m16_res)
                factor("kc16", b, t_mm, bk, (g_end16, 1.0, "lv", 32), False)
                factor("ks32", b, t_mm, bk, (g_end32, 1.0, "lv", 16), False)
                factor.src = None
                stg, sres = fstg["ku"], fstg_res["ku"]

                def f_ku(e, b=b, stg=stg):
                    return e.tensor_scalar(stg[:], kstg[:], vcol(V_U, b),
                                           None, ALU.mult)
                t_ku = sched.add("v", f_ku, deps=[t_kk] + sres.wdeps())
                ksres.did_read(t_ku)
                sres.did_write(t_ku)
                t_kus = sched.add("g", (lambda e, b=b, tb=tb, stg=stg: [
                    e.dma_start(out=P_ku[b * 128:(b + 1) * 128,
                                         tb * 512:(tb + 1) * 512],
                                in_=stg[:])]), deps=[t_ku], dma=1)
                sres.did_read(t_kus)

            t_xv = mk_slab(V_MUV, 1)
            v_ev = []
            prev_vsp = vsp_res.writer
            for sub in range(4):
                for dh in range(2):
                    bk = 5 + ((sub * 2 + dh) % 2)

                    def f_mm(e, sub=sub, dh=dh, bk=bk, sl=slab[1]):
                        last = None
                        for dinb in range(NFB):
                            last = e.matmul(
                                bank(bk),
                                sl[:, dinb, sub * 128:(sub + 1) * 128],
                                wv_s[:, dinb, dh * 512:(dh + 1) * 512],
                                start=(dinb == 0), stop=(dinb == NFB - 1))
                        return last
                    t_mm = sched.add("t", f_mm, deps=[t_xv] + bres[bk].wdeps())
                    slab_res[1].did_read(t_mm)
                    bres[bk].did_write(t_mm)

                    def f_ev(e, sub=sub, dh=dh, bk=bk):
                        return e.tensor_copy(
                            vsp[:, sub, dh * 512:(dh + 1) * 512], bank(bk))
                    t_ev = sched.add("v", f_ev,
                                     deps=[t_mm] + ([prev_vsp] if prev_vsp else []))
                    bres[bk].did_read(t_ev)
                    v_ev.append(t_ev)
            t_vsp = sched.add("g", (lambda e, tb=tb: [
                e.dma_start(
                    out=P_v[tb * 512:(tb + 1) * 512, :]
                    .rearrange("(s p) d -> p s d", p=128),
                    in_=vsp[:])]), deps=v_ev, dma=1)
            vsp_res.did_write(t_vsp)

            t_xg = mk_slab(V_MUG, 0)
            g_ev = []
            prev_gsp = gsp_res.writer
            for sub in range(4):
                for dh in range(2):
                    bk = 5 + ((sub * 2 + dh) % 2)

                    def f_mm(e, sub=sub, dh=dh, bk=bk, sl=slab[0]):
                        last = None
                        for dinb in range(NFB):
                            last = e.matmul(
                                bank(bk),
                                sl[:, dinb, sub * 128:(sub + 1) * 128],
                                wg_s[:, dinb, dh * 512:(dh + 1) * 512],
                                start=(dinb == 0), stop=(dinb == NFB - 1))
                        return last
                    t_mm = sched.add("t", f_mm, deps=[t_xg] + bres[bk].wdeps())
                    slab_res[0].did_read(t_mm)
                    bres[bk].did_write(t_mm)

                    def f_ev(e, sub=sub, dh=dh, bk=bk):
                        return e.activation(
                            gsp[:, sub, dh * 512:(dh + 1) * 512], bank(bk),
                            AF.Silu)
                    t_ev = sched.add("a", f_ev,
                                     deps=[t_mm] + ([prev_gsp] if prev_gsp else []))
                    bres[bk].did_read(t_ev)
                    g_ev.append(t_ev)
            t_gsp = sched.add("g", (lambda e, tb=tb: [
                e.dma_start(
                    out=P_g[tb * 512:(tb + 1) * 512, :]
                    .rearrange("(s p) d -> p s d", p=128),
                    in_=gsp[:])]), deps=g_ev, dma=1)
            gsp_res.did_write(t_gsp)

            rin_s, rin_res = slab[1], slab_res[1]

            def f_rin(e, uf=uf, rin_s=rin_s):
                last = None
                for fb in range(NFB):
                    last = e.tensor_scalar(
                        rin_s[:, fb, :], uf[:, fb, :],
                        vcol(V_RSC, fb), 0.03125, ALU.mult, ALU.mult)
                return last
            t_rin = sched.add("v", f_rin,
                              deps=ures.rdeps() + [t_const] + rin_res.wdeps())
            ures.did_read(t_rin)
            rin_res.did_write(t_rin)

            def f_rmm(e):
                last = None
                for dinb in range(NFB):
                    last = e.matmul(PS[0:8, 7 * 512:8 * 512],
                                    rw_s[:, dinb, :], rin_s[:, dinb, :],
                                    start=(dinb == 0), stop=(dinb == NFB - 1))
                return last
            t_rmm = sched.add("t", f_rmm, deps=[t_rin, t_wld] + bres[7].wdeps())
            rin_res.did_read(t_rmm)
            bres[7].did_write(t_rmm)

            def f_lev(e):
                return e.tensor_copy(lgs[:], PS[0:8, 7 * 512:8 * 512])
            t_lev = sched.add("v", f_lev, deps=[t_rmm] + lgs_res.wdeps()
                              + ([router_prev[0]] if router_prev[0] else []))
            bres[7].did_read(t_lev)
            lgs_res.did_write(t_lev)

            t_tr_ev = []
            for sub in range(4):
                def f_tr(e, sub=sub):
                    return e.transpose(PS[:, 0:8],
                                       lgs[0:8, sub * 128:(sub + 1) * 128],
                                       c_ident[0:8, 0:8])
                t_tr = sched.add("t", f_tr,
                                 deps=[t_lev, t_ident] + bres[0].wdeps())
                lgs_res.did_read(t_tr)
                bres[0].did_write(t_tr)

                def f_trev(e, sub=sub):
                    return e.tensor_copy(ltm[:, sub, :], PS[:, 0:8])
                t_te = sched.add("v", f_trev, deps=[t_tr], drain=True)
                bres[0].did_read(t_te)
                t_tr_ev.append(t_te)

            def f_router(e):
                e.reduce_max(rt_st[:, 0:4], ltm[:], axis=AX.X)
                e.drain()
                mx = rt_st[:, 0:4].unsqueeze(2).broadcast_to([128, 4, E])
                e.tensor_tensor(rt_a[:], ltm[:], mx, ALU.subtract)
                return None
            t_r1 = sched.add("v", f_router, deps=t_tr_ev, drain=True)

            def f_rexp(e):
                return e.activation(rt_b[:], rt_a[:], AF.Exp)
            t_r2 = sched.add("a", f_rexp, deps=[t_r1])

            def f_r3(e):
                e.reduce_sum(rt_st[:, 4:8], rt_b[:], axis=AX.X)
                e.drain()
                e.reciprocal(rt_st[:, 8:12], rt_st[:, 4:8])
                e.drain()
                sm = rt_st[:, 8:12].unsqueeze(2).broadcast_to([128, 4, E])
                e.tensor_tensor(rt_b[:], rt_b[:], sm, ALU.mult)
                mx = rt_st[:, 0:4].unsqueeze(2).broadcast_to([128, 4, E])
                e.tensor_tensor(rt_a[:], ltm[:], mx, ALU.is_ge)
                e.drain()
                e.tensor_copy(rt_u[:], rt_a[:])
                e.drain()
                nf = c_neginf[:].unsqueeze(1).broadcast_to([128, 4, E])
                e.select(rt_c[:], rt_u[:], nf, ltm[:])
                e.drain()
                e.reduce_max(rt_st[:, 12:16], rt_c[:], axis=AX.X)
                e.drain()
                m2 = rt_st[:, 12:16].unsqueeze(2).broadcast_to([128, 4, E])
                e.tensor_tensor(rt_c[:], rt_c[:], m2, ALU.is_ge)
                e.drain()
                e.tensor_tensor(rt_a[:], rt_a[:], rt_c[:], ALU.add)
                e.drain()
                e.tensor_tensor(rt_a[:], rt_a[:], rt_b[:], ALU.mult)
                e.drain()
                e.reduce_sum(rt_st[:, 4:8], rt_a[:], axis=AX.X)
                e.drain()
                e.reciprocal(rt_st[:, 8:12], rt_st[:, 4:8])
                e.drain()
                rn = rt_st[:, 8:12].unsqueeze(2).broadcast_to([128, 4, E])
                e.tensor_tensor(rt_a[:], rt_a[:], rn, ALU.mult)
                e.drain()
                pes = c_pes[:, :].unsqueeze(1).broadcast_to([128, 4, E])
                return e.tensor_tensor(rt_a[:], rt_a[:], pes, ALU.mult)
            t_r3 = sched.add("v", f_r3, deps=[t_r2, t_ninf, t_const],
                             drain=True)

            t_csp = sched.add("g", (lambda e, tb=tb: [
                e.dma_start(out=P_ct[tb * 512:(tb + 1) * 512, :]
                            .rearrange("(s p) e -> p s e", p=128),
                            in_=rt_a[:])]), deps=[t_r3], dma=1)
            router_prev[0] = t_csp
            cfm_res.did_write(t_csp)
            cfm_res.did_read(t_csp)

            def f_lcar(e):
                return e.tensor_copy(Ls[:, :, 0:1], Ls[:, :, 512:513])
            t_lc = sched.add("v", f_lcar, deps=Ls_res.wdeps(), drain=True)
            Ls_res.did_write(t_lc)
            p1_last.append(t_csp)

        ph01.close()
        cur[0] = None
        phase_barrier()

        ph2 = contextlib.ExitStack()
        cur[0] = ph2
        c_mlow = sbuf("c_mlow", (128, 512), U8)
        c_mhi = sbuf("c_mhi", (128, 512), U8)
        c_mD1 = sbuf("c_mD1", (128, 512), U8)
        c_mD2 = sbuf("c_mD2", (128, 512), U8)
        c_gnw = sbuf("c_gnw", (128, D), BF16)
        c_gnb = sbuf("c_gnb", (128, D), BF16)
        t_const2 = sched.add("g", lambda e: [
            e.dma_start(out=c_mlow[:], in_=m_low_i),
            e.dma_start(out=c_mhi[:], in_=m_hi_i),
            e.dma_start(out=c_mD1[:], in_=m_D1_i),
            e.dma_start(out=c_mD2[:], in_=m_D2_i),
            e.dma_start(out=c_gnw[:], in_=gnw_rep),
            e.dma_start(out=c_gnb[:], in_=gnb_rep),
        ], dma=6)
        PB = 256
        NPB = T // PB
        SL = ("rt16", "kt16", "rc16", "kc16", "ro32", "ku", "rr")
        sl_t = {n: [sbuf(f"p2_{n}{i}", (128, NFB, PB),
                         BF16 if n in ("rt16", "kt16") else FP16)
                    for i in range(2)] for n in SL}
        sl_res = {n: [Res(f"p2_{n}{i}") for i in range(2)] for n in SL}
        vt_t = [sbuf(f"p2_v{i}", (128, 2, D), FP16) for i in range(2)]
        vt_res = [Res(f"p2_v{i}") for i in range(2)]
        vs_t = [sbuf(f"p2_vs{i}", (128, 2, D), FP16) for i in range(2)]
        vs_res = [Res(f"p2_vs{i}") for i in range(2)]
        gt_t = [sbuf(f"p2_g{i}", (128, 2, D), FP16) for i in range(2)]
        gt_res = [Res(f"p2_g{i}") for i in range(2)]
        ks_t = [sbuf(f"p2_ks{i}", (128, 2, D), FP16) for i in range(2)]
        ks_res = [Res(f"p2_ks{i}") for i in range(2)]
        dc_t = [sbuf(f"p2_dc{i}", (128, NFB, PB // C), F32) for i in range(2)]
        dc_res = [Res(f"p2_dc{i}") for i in range(2)]
        S32 = sbuf("S32", (128, 512), F32)
        S16 = sbuf("S16", (128, 512), FP16)
        S_res = Res("S")
        t_s0 = sched.add("v", lambda e: (e.memset(S32[:], 0.0),
                                         e.memset(S16[:], 0.0))[-1],
                         drain=True)
        S_res.did_write(t_s0)
        A1_s = sbuf("A1_s", (128, 512), BF16)
        A2_s = sbuf("A2_s", (128, 512), BF16)
        A_res = Res("A")
        c_zero16 = sbuf("c_zero16", (128, 512), BF16)
        t_z16 = sched.add("v", lambda e: e.memset(c_zero16[:], 0.0))
        o32 = sbuf("o32", (128, D), F32)
        osq = sbuf("osq", (128, D), F32)
        ost = sbuf("ost", (128, 64), F32)
        ogt = [sbuf(f"ogt{i}", (128, 2, D), FP16) for i in range(2)]
        ogt_res = [Res(f"ogt{i}") for i in range(2)]
        o_res = Res("o32")

        p2_done = []
        for pb in range(NPB):
            bi = pb % 2
            def f_ld1(e, pb=pb, bi=bi):
                out = []
                for n in SL:
                    out.append(e.dma_start(
                        out=sl_t[n][bi][:],
                        in_=fplane[n].rearrange("(b p) t -> p b t", p=128)
                        [:, :, pb * PB:(pb + 1) * PB]))
                out.append(e.dma_start(
                    out=dc_t[bi][:],
                    in_=P_dec.rearrange("(b p) c -> p b c", p=128)
                    [:, :, pb * (PB // C):(pb + 1) * (PB // C)]))
                return out
            dep_ld = [r for n in SL for r in sl_res[n][bi].wdeps()] + \
                     dc_res[bi].wdeps()
            t_ld1 = sched.add("q", f_ld1, deps=dep_ld, dma=8)
            for n in SL:
                sl_res[n][bi].did_write(t_ld1)
            dc_res[bi].did_write(t_ld1)

            def f_ld2(e, pb=pb, bi=bi):
                out = [
                    e.dma_start(out=vt_t[bi][:],
                                in_=P_v[pb * PB:(pb + 1) * PB, :]
                                .rearrange("(s p) d -> p s d", p=128)),
                    e.dma_start(out=gt_t[bi][:],
                                in_=P_g[pb * PB:(pb + 1) * PB, :]
                                .rearrange("(s p) d -> p s d", p=128)),
                ]
                for sv_ in range(2):
                    st_ = pb * PB + sv_ * 128 + 16
                    nr = min(128, T - st_)
                    out.append(e.dma_start(
                        out=vs_t[bi][0:nr, sv_, :],
                        in_=P_v[st_:st_ + nr, :]))
                return out
            t_ld2 = sched.add("g", f_ld2,
                              deps=vt_res[bi].wdeps() + gt_res[bi].wdeps()
                              + vs_res[bi].wdeps(), dma=4)
            vt_res[bi].did_write(t_ld2)
            gt_res[bi].did_write(t_ld2)
            vs_res[bi].did_write(t_ld2)

            def f_ld3(e, pb=pb, bi=bi):
                return [e.dma_start(out=ks_t[bi][:, s, :],
                                    in_=P_ks32[:, pb * PB + s * 128:
                                               pb * PB + (s + 1) * 128],
                                    transpose=True)
                        for s in range(2)]
            t_ld3 = sched.add("q", f_ld3, deps=ks_res[bi].wdeps(), dma=2)
            ks_res[bi].did_write(t_ld3)

            for grp in range(2):
                gbase = pb * PB + grp * 128
                def f_ad(e, bi=bi, grp=grp, par=0):
                    last = None
                    for cl in range(4):
                        rb = 32 * cl
                        cc = grp * 128 + cl * 32
                        for h in range(par, H, 2):
                            fb, pbase = h // 2, (h % 2) * 64
                            kt = sl_t["kt16"][bi][pbase:pbase + 64, fb, :]
                            rt = sl_t["rt16"][bi][pbase:pbase + 64, fb, :]
                            kc = sl_t["kc16"][bi][pbase:pbase + 64, fb, :]
                            rc = sl_t["rc16"][bi][pbase:pbase + 64, fb, :]
                            ku_ = sl_t["ku"][bi][pbase:pbase + 64, fb, :]
                            rr_ = sl_t["rr"][bi][pbase:pbase + 64, fb, :]
                            tp = (pbase, rb)
                            hc = h * 32
                            e.matmul(PS[rb:rb + 16, 0 * 512 + hc:0 * 512 + hc + 16],
                                     kt[:, cc:cc + 16], rt[:, cc:cc + 16],
                                     start=True, stop=True, tile_position=tp,
                                     skip_group_check=True)
                            e.matmul(PS[rb:rb + 16, 1 * 512 + hc + 16:1 * 512 + hc + 32],
                                     kt[:, cc + 16:cc + 32], rt[:, cc + 16:cc + 32],
                                     start=True, stop=True, tile_position=tp,
                                     skip_group_check=True)
                            e.matmul(PS[rb:rb + 16, 0 * 512 + hc + 16:0 * 512 + hc + 32],
                                     kc[:, cc:cc + 16], rc[:, cc + 16:cc + 32],
                                     start=True, stop=True, tile_position=tp,
                                     skip_group_check=True)
                            e.matmul(PS[rb:rb + 16, 2 * 512 + hc:2 * 512 + hc + 16],
                                     ku_[:, cc:cc + 16], rr_[:, cc:cc + 16],
                                     start=True, stop=True, tile_position=tp,
                                     skip_group_check=True)
                            last = e.matmul(
                                PS[rb:rb + 16, 2 * 512 + hc + 16:2 * 512 + hc + 32],
                                ku_[:, cc + 16:cc + 32], rr_[:, cc + 16:cc + 32],
                                start=True, stop=True, tile_position=tp,
                                skip_group_check=True)
                    return last
                adeps = [t_ld1] + bres[0].wdeps() + bres[1].wdeps() + bres[2].wdeps()
                t_ad0 = sched.add("t", (lambda e, bi=bi, grp=grp: f_ad(e, bi, grp, 0)),
                                  deps=adeps)
                t_ad1 = sched.add("t", (lambda e, bi=bi, grp=grp: f_ad(e, bi, grp, 1)),
                                  deps=[t_ld1], hard=[t_ad0])
                for n in ("kt16", "rt16", "kc16", "rc16", "ku", "rr"):
                    sl_res[n][bi].did_read(t_ad0)
                    sl_res[n][bi].did_read(t_ad1)
                for b_ in (0, 1, 2):
                    bres[b_].did_write(t_ad1)

                def f_ev1(e):
                    e.select(A1_s[:], c_mlow[:], PS[:, 0:512], c_zero16[:])
                    e.drain()
                    return e.copy_predicated(A1_s[:], c_mD1[:],
                                             PS[:, 2 * 512:3 * 512])
                t_e1 = sched.add("v", f_ev1, deps=[t_ad1, t_z16, t_const2]
                                 + A_res.wdeps(), drain=True)

                def f_ev2(e):
                    e.select(A2_s[:], c_mhi[:], PS[:, 512:1024], c_zero16[:])
                    e.drain()
                    return e.copy_predicated(A2_s[:], c_mD2[:],
                                             PS[:, 2 * 512:3 * 512])
                t_e2 = sched.add("v", f_ev2, deps=[t_ad1, t_z16, t_const2]
                                 + A_res.wdeps(), drain=True)
                A_res.did_write(t_e2)
                for b_ in (0, 1, 2):
                    bres[b_].did_read(t_e1)
                    bres[b_].did_read(t_e2)

                for cl in range(4):
                    rb = 32 * cl
                    c_glob = (gbase // 32) + cl
                    sv = grp
                    prev_merge = S_res.writer

                    def f_av(e, bi=bi, rb=rb, sv=sv, cl=cl):
                        last = None
                        for h in range(H):
                            hc = h * 32
                            vlo = vt_t[bi][rb:rb + 32, sv,
                                           h * 64:(h + 1) * 64][0:16, :]
                            vhi = vs_t[bi][rb:rb + 16, sv, h * 64:(h + 1) * 64]
                            ob = 3 * 512 + (h % 2) * 512 + (h // 2) * 64
                            e.matmul(PS[rb:rb + 32, ob:ob + 64],
                                     A1_s[rb:rb + 16, hc:hc + 32], vlo,
                                     start=True, stop=False,
                                     tile_position=(rb, rb),
                                     skip_group_check=True)
                            last = e.matmul(
                                PS[rb:rb + 32, ob:ob + 64],
                                A2_s[rb:rb + 16, hc:hc + 32], vhi,
                                start=False, stop=True,
                                tile_position=(rb, rb),
                                skip_group_check=True)
                        return last
                    avdeps = [t_e2, t_ld2] + \
                        (bres[3].wdeps() + bres[4].wdeps() if cl == 0 else [])
                    if "av" in skip:
                        f_av = (lambda e: None)
                    t_av = sched.add("t", f_av, deps=avdeps)
                    A_res.did_read(t_av)
                    vt_res[bi].did_read(t_av)
                    vs_res[bi].did_read(t_av)
                    bres[3].did_write(t_av)
                    bres[4].did_write(t_av)

                    def f_oi(e, bi=bi, rb=rb, cl=cl, grp=grp):
                        last = None
                        for h in range(H):
                            pbase = (h % 2) * 64
                            fb = h // 2
                            cc = grp * 128 + cl * 32
                            ro = sl_t["ro32"][bi][pbase:pbase + 64, fb,
                                                  cc:cc + 32]
                            ob = 5 * 512 + (h % 2) * 512 + (h // 2) * 64
                            last = e.matmul(
                                PS[rb:rb + 32, ob:ob + 64], ro,
                                S16[pbase:pbase + 64,
                                    (h // 2) * 64:(h // 2) * 64 + 64],
                                start=True, stop=True,
                                tile_position=(pbase, rb),
                                skip_group_check=True)
                        return last
                    oideps = [t_ld1] + S_res.rdeps() + \
                        (bres[5].wdeps() + bres[6].wdeps() if cl == 0 else [])
                    if "oi" in skip:
                        f_oi = (lambda e: None)
                    t_oi = sched.add("t", f_oi, deps=oideps, hard=[t_av])
                    sl_res["ro32"][bi].did_read(t_oi)
                    S_res.did_read(t_oi)
                    bres[5].did_write(t_oi)
                    bres[6].did_write(t_oi)

                    def f_su(e, bi=bi, rb=rb, sv=sv):
                        last = None
                        for h in range(H):
                            par = h % 2
                            hk = ks_t[bi][rb:rb + 32, sv,
                                          h * 64:(h + 1) * 64]
                            vv = vt_t[bi][rb:rb + 32, sv,
                                          h * 64:(h + 1) * 64]
                            last = e.matmul(
                                PS[par * 64:par * 64 + 64,
                                   7 * 512 + (h // 2) * 64:
                                   7 * 512 + (h // 2) * 64 + 64],
                                hk, vv, start=True, stop=True,
                                tile_position=(rb, par * 64),
                                skip_group_check=True)
                        return last
                    sudeps = [t_ld3, t_ld2] + bres[7].wdeps() + \
                        ([prev_merge] if prev_merge else [])
                    t_su = sched.add("t", f_su, deps=sudeps, hard=[t_oi])
                    ks_res[bi].did_read(t_su)
                    vt_res[bi].did_read(t_su)
                    bres[7].did_write(t_su)

                    def f_merge(e, bi=bi, c_glob=c_glob):
                        cloc = c_glob % (PB // C)
                        db = dc_t[bi][:, :, cloc].unsqueeze(2) \
                            .broadcast_to([128, NFB, 64])
                        e.tensor_tensor(
                            S32[:].rearrange("p (g v) -> p g v", g=NFB),
                            S32[:].rearrange("p (g v) -> p g v", g=NFB),
                            db, ALU.mult)
                        e.drain()
                        e.tensor_tensor(S32[:], S32[:], PS[:, 7 * 512:8 * 512],
                                        ALU.add)
                        e.drain()
                        return e.tensor_copy(S16[:], S32[:])
                    t_mg = sched.add("v", f_merge,
                                     deps=[t_su, t_ld1] + S_res.wdeps(),
                                     drain=True)
                    dc_res[bi].did_read(t_mg)
                    bres[7].did_read(t_mg)
                    S_res.did_write(t_mg)

                t_last_av = t_av
                t_last_oi = t_oi

                def f_oev(e):
                    if "oi" in skip:
                        e.tensor_copy(o32[:, 0:512], PS[:, 3 * 512:4 * 512])
                        e.drain()
                        return e.tensor_copy(o32[:, 512:1024],
                                             PS[:, 4 * 512:5 * 512])
                    if "av" in skip:
                        e.tensor_copy(o32[:, 0:512], PS[:, 5 * 512:6 * 512])
                        e.drain()
                        return e.tensor_copy(o32[:, 512:1024],
                                             PS[:, 6 * 512:7 * 512])
                    e.tensor_copy(o32[:, 0:512], PS[:, 3 * 512:4 * 512])
                    e.tensor_copy(o32[:, 512:1024], PS[:, 4 * 512:5 * 512])
                    e.drain()
                    e.tensor_tensor(o32[:, 0:512], o32[:, 0:512],
                                    PS[:, 5 * 512:6 * 512], ALU.add)
                    e.drain()
                    return e.tensor_tensor(o32[:, 512:1024], o32[:, 512:1024],
                                           PS[:, 6 * 512:7 * 512], ALU.add)
                t_oev = sched.add("v", f_oev,
                                 deps=[t_last_av, t_last_oi] + o_res.wdeps(),
                                 drain=True)
                o_res.did_write(t_oev)
                for b_ in (3, 4, 5, 6):
                    bres[b_].did_read(t_oev)

                def f_gn1(e):
                    ov = o32[:].rearrange("p (s v) -> p s v", v=64)
                    e.reduce_sum(ost[:, 0:16], ov, axis=AX.X)
                    e.drain()
                    e.tensor_scalar(ost[:, 0:16], ost[:, 0:16], 1.0 / 64,
                                    None, ALU.mult)
                    e.drain()
                    e.tensor_tensor(osq[:], o32[:], o32[:], ALU.mult)
                    e.drain()
                    e.reduce_sum(ost[:, 16:32], osq[:].rearrange(
                        "p (s v) -> p s v", v=64), axis=AX.X)
                    e.drain()
                    e.tensor_scalar(ost[:, 16:32], ost[:, 16:32], 1.0 / 64,
                                    None, ALU.mult)
                    e.drain()
                    e.tensor_tensor(ost[:, 32:48], ost[:, 0:16], ost[:, 0:16],
                                    ALU.mult)
                    e.drain()
                    return e.tensor_tensor(ost[:, 16:32], ost[:, 16:32],
                                           ost[:, 32:48], ALU.subtract)
                if dbg_raw_o:
                    t_odbg = sched.add("g", (lambda e, gb2_=gbase: [
                        e.dma_start(out=P_odbg[gb2_:gb2_ + 128, :],
                                    in_=o32[:])]), deps=[t_oev], dma=1)
                    o_res.did_read(t_odbg)
                t_gn1 = sched.add("v", f_gn1, deps=[t_oev], drain=True)
                o_res.did_read(t_gn1)

                def f_gn2(e):
                    e.activation(ost[:, 48:64], ost[:, 16:32], AF.Ln,
                                 bias=c_eps[:, 2:3])
                    e.drain()
                    return e.activation(ost[:, 48:64], ost[:, 48:64], AF.Exp,
                                        scale=-0.5)
                t_gn2 = sched.add("a", f_gn2, deps=[t_gn1, t_memc], drain=True)

                def f_gn3(e, bi=bi, grp=grp, og=ogt[bi]):
                    ov = o32[:].rearrange("p (s v) -> p s v", v=64)
                    mb = ost[:, 0:16].unsqueeze(2).broadcast_to([128, 16, 64])
                    rb_ = ost[:, 48:64].unsqueeze(2).broadcast_to([128, 16, 64])
                    e.tensor_tensor(ov, ov, mb, ALU.subtract)
                    e.drain()
                    e.tensor_tensor(ov, ov, rb_, ALU.mult)
                    e.drain()
                    e.tensor_tensor(o32[:], o32[:], c_gnw[:], ALU.mult)
                    e.drain()
                    e.tensor_tensor(o32[:], o32[:], c_gnb[:], ALU.add)
                    e.drain()
                    gperm = gt_t[bi][:, grp, :].rearrange(
                        "p (hp pa v) -> p pa hp v", hp=8, pa=2, v=64)
                    ogperm = og[:, grp, :].rearrange(
                        "p (hp pa v) -> p pa hp v", hp=8, pa=2, v=64)
                    o4 = o32[:].rearrange("p (pa hp v) -> p pa hp v",
                                          pa=2, hp=8, v=64)
                    return e.tensor_tensor(ogperm, o4, gperm, ALU.mult)
                gn3deps = [t_gn2, t_const2]
                if grp == 0:
                    gn3deps += ogt_res[bi].wdeps()
                t_gn3 = sched.add("v", f_gn3, deps=gn3deps, drain=True)
                o_res.did_read(t_gn3)
                gt_res[bi].did_read(t_gn3)
                if grp == 1:
                    ogt_res[bi].did_write(t_gn3)

            def f_osp(e, pb=pb, bi=bi):
                return [e.dma_start(
                    out=P_o[pb * PB:(pb + 1) * PB, :]
                    .rearrange("(s p) d -> p s d", p=128), in_=ogt[bi][:])]
            t_osp = sched.add("g", f_osp, deps=[t_gn3], dma=1)
            ogt_res[bi].did_read(t_osp)
            p2_done.append(t_osp)

        ph2.close()
        cur[0] = None
        phase_barrier()

        ph3a = contextlib.ExitStack()
        cur[0] = ph3a
        c_ln2 = sbuf("c_ln2", (128, D), BF16)
        t_const3 = sched.add("g", lambda e: [
            e.dma_start(out=c_ln2[:], in_=ln2w_rep)], dma=1)
        wo_s = sbuf("wo_s", (128, NFB, D), FP16)
        t_wo = sched.add("g", lambda e: [
            e.dma_start(out=wo_s[:], in_=Wo.rearrange("(b p) o -> p b o", p=128))
        ], dma=1)
        ofm = [sbuf(f"ofm{i}", (128, NFB, 512), FP16) for i in range(2)]
        ofm_res = [Res(f"ofm{i}") for i in range(2)]
        xns = [sbuf(f"xns{i}", (128, 4, D), FP16) for i in range(2)]
        xns_res = [Res(f"xns{i}") for i in range(2)]
        xasl = sbuf("xasl", (128, 4, D), F32)
        xa_res = Res("xa")
        h2sl = sbuf("h2sl", (128, 4, D), FP16)
        h2_res = Res("h2")
        p3st = sbuf("p3st", (128, 8), F32)
        p3st_res = Res("p3st")

        for tb in range(NTB):
            bi = tb % 2
            t_ol = sched.add("q", (lambda e, tb=tb, bi=bi: [
                e.dma_start(out=ofm[bi][:, fb, :],
                            in_=P_o[tb * 512:(tb + 1) * 512,
                                    fb * 128:(fb + 1) * 128], transpose=True)
                for fb in range(NFB)]), deps=ofm_res[bi].wdeps(), dma=NFB)
            ofm_res[bi].did_write(t_ol)
            t_xl = sched.add("g", (lambda e, tb=tb, bi=bi: [
                e.dma_start(out=xns[bi][:],
                            in_=P_xn[tb * 512:(tb + 1) * 512, :]
                            .rearrange("(s p) d -> p s d", p=128))]),
                deps=xns_res[bi].wdeps(), dma=1)
            xns_res[bi].did_write(t_xl)

            for sub in range(4):
                for dh in range(2):
                    bk = (sub * 2 + dh) % 4

                    def f_mm(e, bi=bi, sub=sub, dh=dh, bk=bk):
                        last = None
                        for dinb in range(NFB):
                            last = e.matmul(
                                bank(bk),
                                ofm[bi][:, dinb, sub * 128:(sub + 1) * 128],
                                wo_s[:, dinb, dh * 512:(dh + 1) * 512],
                                start=(dinb == 0), stop=(dinb == NFB - 1))
                        return last
                    t_mm = sched.add("t", f_mm,
                                     deps=[t_ol, t_wo] + bres[bk].wdeps())
                    ofm_res[bi].did_read(t_mm)
                    bres[bk].did_write(t_mm)

                    def f_ev(e, bi=bi, sub=sub, dh=dh, bk=bk):
                        return e.tensor_tensor(
                            xasl[:, sub, dh * 512:(dh + 1) * 512], bank(bk),
                            xns[bi][:, sub, dh * 512:(dh + 1) * 512], ALU.add)
                    t_ev = sched.add("v", f_ev,
                                     deps=[t_mm, t_xl] + (xa_res.wdeps()
                                                          if sub == 0 and dh == 0
                                                          else []))
                    bres[bk].did_read(t_ev)
                    xns_res[bi].did_read(t_ev)
                    if sub == 3 and dh == 1:
                        xa_res.did_write(t_ev)
                    t_last_ev = t_ev

            t_xasp = sched.add("g", (lambda e, tb=tb: [
                e.dma_start(out=P_xa[tb * 512:(tb + 1) * 512, :]
                            .rearrange("(s p) d -> p s d", p=128),
                            in_=xasl[:])]), deps=[t_last_ev], dma=1)
            xa_res.did_read(t_xasp)

            for sub in range(4):
                def f_ss(e, sub=sub):
                    return e.activation(bank(4, 1024), xasl[:, sub, :],
                                        AF.Square, accum_out=p3st[:, 0:1])
                t_ss = sched.add("a", f_ss, deps=[t_last_ev] + p3st_res.wdeps())
                xa_res.did_read(t_ss)
                p3st_res.did_write(t_ss)
                bres[4].did_write(t_ss)
                bres[5].did_write(t_ss)

                def f_s2(e):
                    e.activation(p3st[:, 1:2], p3st[:, 0:1], AF.Ln,
                                 bias=c_eps[:, 1:2], scale=1.0 / D)
                    e.drain()
                    return e.activation(p3st[:, 1:2], p3st[:, 1:2], AF.Exp,
                                        scale=-0.5)
                t_s2 = sched.add("a", f_s2, deps=[t_ss, t_memc], drain=True)

                def f_h2(e, sub=sub):
                    e.tensor_scalar(h2sl[:, sub, :], xasl[:, sub, :],
                                    p3st[:, 1:2], None, ALU.mult)
                    e.drain()
                    return e.tensor_tensor(h2sl[:, sub, :], h2sl[:, sub, :],
                                           c_ln2[:], ALU.mult)
                t_h2 = sched.add("v", f_h2,
                                 deps=[t_s2, t_const3] + (h2_res.wdeps()
                                                          if sub == 0 else []),
                                 drain=True)
                xa_res.did_read(t_h2)
                p3st_res.did_read(t_h2)
                if sub == 3:
                    h2_res.did_write(t_h2)

            t_h2sp = sched.add("g", (lambda e, tb=tb: [
                e.dma_start(out=P_h2[tb * 512:(tb + 1) * 512, :]
                            .rearrange("(s p) d -> p s d", p=128),
                            in_=h2sl[:])]), deps=[t_h2], dma=1)
            h2_res.did_read(t_h2sp)
            p3a_last = t_h2sp

        ph3a.close()
        cur[0] = None
        phase_barrier()

        ph3b = contextlib.ExitStack()
        cur[0] = ph3b
        w1_s = sbuf("w1_s", (128, NFB, FF), FP16)
        w2_s = sbuf("w2_s", (128, FF // 128, D), FP16)
        t_w12 = sched.add("g", lambda e: [
            e.dma_start(out=w1_s[:], in_=ffn_w1.rearrange("(b p) o -> p b o", p=128)),
            e.dma_start(out=w2_s[:], in_=ffn_w2.rearrange("(b p) o -> p b o", p=128)),
        ], dma=2)
        h2fm_t = sbuf("h2fm", (128, NFB, 512), FP16)
        h2fm = [h2fm_t, h2fm_t]
        _hr = Res("h2fm")
        h2fm_res = [_hr, _hr]
        gelu_sl = sbuf("gelu_sl", (128, FF // 128, 512), FP16)
        gelu_res = Res("gelu")
        xas2_t = sbuf("xas2", (128, 4, D), F32)
        xas2 = [xas2_t, xas2_t]
        _xr = Res("xas2")
        xas2_res = [_xr, _xr]
        ssl = sbuf("ssl", (128, 4, D), F32)
        ssl_res = Res("ssl")

        for tb in range(NTB):
            bi = tb % 2
            t_hl = sched.add("q", (lambda e, tb=tb, bi=bi: [
                e.dma_start(out=h2fm[bi][:, fb, :],
                            in_=P_h2[tb * 512:(tb + 1) * 512,
                                     fb * 128:(fb + 1) * 128], transpose=True)
                for fb in range(NFB)]), deps=h2fm_res[bi].wdeps(), dma=NFB)
            h2fm_res[bi].did_write(t_hl)
            t_xal = sched.add("g", (lambda e, tb=tb, bi=bi: [
                e.dma_start(out=xas2[bi][:],
                            in_=P_xa[tb * 512:(tb + 1) * 512, :]
                            .rearrange("(s p) d -> p s d", p=128))]),
                deps=xas2_res[bi].wdeps(), dma=1)
            xas2_res[bi].did_write(t_xal)

            gl_tasks = []
            for fc in range(FF // 128):
                bk = fc % 2

                def f_mm(e, bi=bi, fc=fc, bk=bk):
                    last = None
                    for dinb in range(NFB):
                        last = e.matmul(
                            bank(bk), w1_s[:, dinb, fc * 128:(fc + 1) * 128],
                            h2fm[bi][:, dinb, :],
                            start=(dinb == 0), stop=(dinb == NFB - 1))
                    return last
                t_mm = sched.add("t", f_mm, deps=[t_hl, t_w12] + bres[bk].wdeps())
                h2fm_res[bi].did_read(t_mm)
                bres[bk].did_write(t_mm)

                def f_gel(e, fc=fc, bk=bk):
                    return e.activation(gelu_sl[:, fc, :], bank(bk), AF.Gelu)
                t_gel = sched.add("a", f_gel,
                                  deps=[t_mm] + (gelu_res.wdeps() if fc == 0
                                                 else []))
                bres[bk].did_read(t_gel)
                gl_tasks.append(t_gel)
            gelu_res.did_write(gl_tasks[-1])

            for sub in range(4):
                for dh in range(2):
                    bk = 2 + (sub * 2 + dh) % 2

                    def f_mm(e, sub=sub, dh=dh, bk=bk):
                        last = None
                        for fc in range(FF // 128):
                            last = e.matmul(
                                bank(bk),
                                gelu_sl[:, fc, sub * 128:(sub + 1) * 128],
                                w2_s[:, fc, dh * 512:(dh + 1) * 512],
                                start=(fc == 0), stop=(fc == FF // 128 - 1))
                        return last
                    t_mm2 = sched.add("t", f_mm,
                                      deps=gl_tasks + [t_w12] + bres[bk].wdeps())
                    gelu_res.did_read(t_mm2)
                    bres[bk].did_write(t_mm2)

                    def f_ev(e, bi=bi, sub=sub, dh=dh, bk=bk):
                        return e.tensor_tensor(
                            ssl[:, sub, dh * 512:(dh + 1) * 512], bank(bk),
                            xas2[bi][:, sub, dh * 512:(dh + 1) * 512], ALU.add)
                    t_ev2 = sched.add("v", f_ev,
                                      deps=[t_mm2, t_xal] + (ssl_res.wdeps()
                                                             if sub == 0 and dh == 0
                                                             else []))
                    bres[bk].did_read(t_ev2)
                    xas2_res[bi].did_read(t_ev2)
            ssl_res.did_write(t_ev2)
            t_ssp = sched.add("g", (lambda e, tb=tb: [
                e.dma_start(out=P_s[tb * 512:(tb + 1) * 512, :]
                            .rearrange("(s p) d -> p s d", p=128),
                            in_=ssl[:])]), deps=[t_ev2], dma=1)
            ssl_res.did_read(t_ssp)

        ph3b.close()
        cur[0] = None
        phase_barrier()

        ph4 = contextlib.ExitStack()
        cur[0] = ph4
        SBT = 2048
        NSB = T // SBT
        xnfm = sbuf("xnfm", (128, NFB, SBT), FP16)
        xnfm_res = Res("xnfm")
        gwt = sbuf("gwt", (128, 2, NFB, HID), FP16)
        gwt_res = Res("gwt")
        lw_s = sbuf("lw_s", (128, HID // 128, D), FP16)
        lw_res = Res("lw")
        act_sl = sbuf("act_sl", (128, HID // 128, 512), FP16)
        act_res = Res("act")
        acc_sl = sbuf("acc_sl", (128, SBT // 128, D), FP16)
        acc_res = Res("acc")
        cc_sl = sbuf("cc_sl", (128, 4, E), F32)
        cc_res = Res("cc_sl")
        evt = sbuf("evt", (128, 512), F32)
        evt_res = Res("evt")

        for sb in range(NSB):
            t_xnl = sched.add("q", (lambda e, sb=sb: [
                e.dma_start(out=xnfm[:, fb, :],
                            in_=P_xn[sb * SBT:(sb + 1) * SBT,
                                     fb * 128:(fb + 1) * 128], transpose=True)
                for fb in range(NFB)]), deps=xnfm_res.wdeps(), dma=NFB)
            xnfm_res.did_write(t_xnl)

            t_acc0 = sched.add("v", lambda e: e.memset(acc_sl[:], 0.0),
                               deps=acc_res.wdeps(), drain=True)
            acc_res.did_write(t_acc0)
            t_lastmac = t_acc0

            for e_ in range(E):
                t_gwl0 = sched.add("g", (lambda e, e_=e_: [
                    e.dma_start(out=gwt[:, 0, :, :],
                                in_=P_gwT[e_, 0].rearrange(
                                    "(b p) h -> p b h", p=128))]),
                    deps=gwt_res.wdeps(), dma=1)
                t_gwl = sched.add("q", (lambda e, e_=e_: [
                    e.dma_start(out=gwt[:, 1, :, :],
                                in_=P_gwT[e_, 1].rearrange(
                                    "(b p) h -> p b h", p=128))]),
                    deps=gwt_res.wdeps(), dma=1)
                gwt_res.did_write(t_gwl0)
                gwt_res.did_write(t_gwl)

                t_lwl = sched.add("g", (lambda e, e_=e_: [
                    e.dma_start(out=lw_s[:],
                                in_=linear_w[e_].rearrange(
                                    "(c p) d -> p c d", p=128))]),
                    deps=lw_res.wdeps(), dma=1)
                lw_res.did_write(t_lwl)

                for tq in range(SBT // 512):
                    tq0 = tq * 512
                    t_cl = sched.add("g", (lambda e, sb=sb, tq0=tq0: [
                        e.dma_start(out=cc_sl[:],
                                    in_=P_ct[sb * SBT + tq0:
                                             sb * SBT + tq0 + 512, :]
                                    .rearrange("(s p) e -> p s e", p=128))
                    ]), deps=cc_res.wdeps(), dma=1)
                    cc_res.did_write(t_cl)

                    at_tasks = []
                    act_war = act_res.wdeps()
                    for hc in range(HID // 128):
                        b0 = hc % 2
                        b1 = 2 + hc % 2

                        def f_g01(e, hc=hc, tq0=tq0, b0=b0, b1=b1):
                            last = None
                            for dinb in range(NFB):
                                e.matmul(bank(b0),
                                         gwt[:, 0, dinb, hc * 128:(hc + 1) * 128],
                                         xnfm[:, dinb, tq0:tq0 + 512],
                                         start=(dinb == 0),
                                         stop=(dinb == NFB - 1))
                            for dinb in range(NFB):
                                last = e.matmul(
                                    bank(b1),
                                    gwt[:, 1, dinb, hc * 128:(hc + 1) * 128],
                                    xnfm[:, dinb, tq0:tq0 + 512],
                                    start=(dinb == 0), stop=(dinb == NFB - 1))
                            return last
                        t_g01 = sched.add("t", f_g01,
                                          deps=[t_gwl0, t_gwl, t_xnl]
                                          + bres[b0].wdeps() + bres[b1].wdeps())
                        gwt_res.did_read(t_g01)
                        xnfm_res.did_read(t_g01)
                        bres[b0].did_write(t_g01)
                        bres[b1].did_write(t_g01)

                        def f_act1(e, b0=b0):
                            return e.activation(evt[:], bank(b0),
                                                AF.Gelu_apprx_tanh)
                        t_a1 = sched.add("a", f_act1,
                                         deps=[t_g01] + evt_res.wdeps())
                        bres[b0].did_read(t_a1)
                        evt_res.did_write(t_a1)

                        def f_act2(e, hc=hc, b1=b1):
                            return e.tensor_tensor(act_sl[:, hc, :], evt[:],
                                                   bank(b1), ALU.mult)
                        t_a2 = sched.add("v", f_act2,
                                         deps=[t_a1] + act_war)
                        evt_res.did_read(t_a2)
                        bres[b1].did_read(t_a2)
                        at_tasks.append(t_a2)
                    act_res.did_write(at_tasks[-1])
                    if dbg_raw_o and sb == 0 and e_ == 0 and tq == 0:
                        t_adbg = sched.add("g", (lambda e: [
                            e.dma_start(out=P_adbg.rearrange(
                                "(c p) t -> p c t", p=128), in_=act_sl[:])]),
                            deps=[at_tasks[-1]], dma=1)
                        act_res.did_read(t_adbg)

                    for dh in range(2):
                        for su in range(4):
                            bk = 4 + su % 2

                            def f_dn(e, su=su, dh=dh, bk=bk):
                                last = None
                                for hc in range(HID // 128):
                                    last = e.matmul(
                                        bank(bk),
                                        act_sl[:, hc, su * 128:(su + 1) * 128],
                                        lw_s[:, hc, dh * 512:(dh + 1) * 512],
                                        start=(hc == 0),
                                        stop=(hc == HID // 128 - 1))
                                return last
                            t_dn = sched.add("t", f_dn,
                                             deps=at_tasks + [t_lwl]
                                             + bres[bk].wdeps())
                            act_res.did_read(t_dn)
                            lw_res.did_read(t_dn)
                            bres[bk].did_write(t_dn)

                            def f_mac(e, su=su, dh=dh, bk=bk, tq0=tq0,
                                      e_=e_):
                                a = acc_sl[:, tq0 // 128 + su,
                                           dh * 512:(dh + 1) * 512]
                                return e.scalar_tensor_tensor(
                                    a, bank(bk), cc_sl[:, su, e_:e_ + 1], a,
                                    ALU.mult, ALU.add)
                            t_mac = sched.add("v", f_mac,
                                              deps=[t_dn, t_cl, t_lastmac],
                                              drain=True)
                            bres[bk].did_read(t_mac)
                            cc_res.did_read(t_mac)
                            t_lastmac = t_mac
            acc_res.did_write(t_lastmac)

            t_msp = sched.add("g", (lambda e, sb=sb: [
                e.dma_start(out=P_m[sb * SBT:(sb + 1) * SBT, :]
                            .rearrange("(s p) d -> p s d", p=128),
                            in_=acc_sl[:])]), deps=[t_lastmac], dma=1)
            acc_res.did_read(t_msp)

        ph4.close()
        cur[0] = None
        phase_barrier()

        ph5 = contextlib.ExitStack()
        cur[0] = ph5
        xfin = [sbuf(f"xfin{i}", (128, 4, D), F32) for i in range(2)]
        xfin_res = [Res(f"xfin{i}") for i in range(2)]
        sfin = [sbuf(f"sfin{i}", (128, 4, D), F32) for i in range(2)]
        sfin_res = [Res(f"sfin{i}") for i in range(2)]
        mfin = [sbuf(f"mfin{i}", (128, 4, D), FP16) for i in range(2)]
        mfin_res = [Res(f"mfin{i}") for i in range(2)]
        gfin = [sbuf(f"gfin{i}", (128, 4, D), F32) for i in range(2)]
        gfin_res = [Res(f"gfin{i}") for i in range(2)]

        for tb in range(NTB):
            bi = tb % 2
            t_xf = sched.add("q", (lambda e, tb=tb, bi=bi: [
                e.dma_start(out=xfin[bi][:],
                            in_=x_in[tb * 512:(tb + 1) * 512, :]
                            .rearrange("(s p) d -> p s d", p=128))]),
                deps=xfin_res[bi].wdeps(), dma=1)
            xfin_res[bi].did_write(t_xf)
            t_sf = sched.add("g", (lambda e, tb=tb, bi=bi: [
                e.dma_start(out=sfin[bi][:],
                            in_=P_s[tb * 512:(tb + 1) * 512, :]
                            .rearrange("(s p) d -> p s d", p=128)),
                e.dma_start(out=mfin[bi][:],
                            in_=P_m[tb * 512:(tb + 1) * 512, :]
                            .rearrange("(s p) d -> p s d", p=128))]),
                deps=sfin_res[bi].wdeps() + mfin_res[bi].wdeps(), dma=2)
            sfin_res[bi].did_write(t_sf)
            mfin_res[bi].did_write(t_sf)

            def f_sil(e, bi=bi):
                return e.activation(gfin[bi][:], mfin[bi][:], AF.Silu)
            t_sil = sched.add("a", f_sil,
                              deps=[t_sf] + gfin_res[bi].wdeps())
            mfin_res[bi].did_read(t_sil)
            gfin_res[bi].did_write(t_sil)

            def f_fin(e, bi=bi):
                e.tensor_tensor(gfin[bi][:], gfin[bi][:], sfin[bi][:], ALU.mult)
                e.drain()
                return e.tensor_tensor(gfin[bi][:], gfin[bi][:], xfin[bi][:],
                                       ALU.add)
            t_fin2 = sched.add("v", f_fin, deps=[t_sil, t_xf, t_sf],
                               drain=True)
            sfin_res[bi].did_read(t_fin2)
            xfin_res[bi].did_read(t_fin2)
            gfin_res[bi].did_write(t_fin2)

            t_out = sched.add("g", (lambda e, tb=tb, bi=bi: [
                e.dma_start(out=out_t[tb * 512:(tb + 1) * 512, :]
                            .rearrange("(s p) d -> p s d", p=128),
                            in_=gfin[bi][:])]), deps=[t_fin2], dma=1)
            gfin_res[bi].did_read(t_out)

        ph5.close()
        cur[0] = None

        prev = None
        for n in debug_taps:
            deps = sched.lasts() if prev is None else [prev]
            prev = sched.add("g", (lambda e, p=planes[n], d=taps[n]:
                                   [e.dma_start(out=d, in_=p)]),
                             deps=deps, dma=1)

        block = stack.enter_context(nc.Block())
        sched.emit(nc, block, sems)

    return nc



def _vec_cols(inputs):
    def col(v):
        return np.asarray(v, np.float32).reshape(NFB, 128).T
    order = [inputs["L0_attn_ln1_w"], inputs["L0_attn_mu_r"],
             inputs["L0_attn_mu_k"], inputs["L0_attn_mu_v"],
             inputs["L0_attn_mu_w"], inputs["L0_attn_mu_g"],
             np.asarray(inputs["L0_attn_u"], np.float32).reshape(-1),
             inputs["L0_ffn_router_scale"]]
    return np.concatenate([col(v) for v in order], axis=1)


def _consts(inputs):
    col = np.arange(512)
    t = col % 32
    j = (np.arange(128) % 32)[:, None]
    valid = j < 16
    m_low = (valid & (j < t[None, :])).astype(np.uint8)
    m_hi = (valid & (j + 16 < t[None, :])).astype(np.uint8)
    m_D1 = (valid & (t[None, :] < 16) & (j == t[None, :])).astype(np.uint8)
    m_D2 = (valid & (t[None, :] >= 16) & (j == t[None, :] - 16)).astype(np.uint8)
    gw = np.asarray(inputs["L0_attn_gn_w"], np.float32).reshape(H, DK)
    gb = np.asarray(inputs["L0_attn_gn_b"], np.float32).reshape(H, DK)
    perm = np.array([2 * hp + pa for pa in range(2) for hp in range(8)])
    gw_p = gw[perm].reshape(1, D)
    gb_p = gb[perm].reshape(1, D)
    return {
        "vec_cols": np.ascontiguousarray(_vec_cols(inputs)),
        "m_low": m_low, "m_hi": m_hi, "m_D1": m_D1, "m_D2": m_D2,
        "gnw_rep": np.ascontiguousarray(np.tile(gw_p, (128, 1))),
        "gnb_rep": np.ascontiguousarray(np.tile(gb_p, (128, 1))),
        "ln2w_rep": np.ascontiguousarray(np.tile(
            np.asarray(inputs["L0_attn_ln2_w"], np.float32).reshape(1, D),
            (128, 1))),
        "pes_rep": np.ascontiguousarray(np.tile(
            np.asarray(inputs["L0_ffn_per_expert_scale"], np.float32
                       ).reshape(1, E), (128, 1))),
        "ident": np.eye(128, dtype=np.float32),
    }


def _in_maps(inputs):
    x = np.ascontiguousarray(np.asarray(inputs["L0_attn_x"], np.float32))
    shared = {k: np.ascontiguousarray(np.asarray(inputs["L0_attn_" + k],
                                                 np.float32))
              for k in ("Wr", "Wk", "Wv", "Wg", "Wo", "w0", "wA", "wB",
                        "ffn_w1", "ffn_w2")}
    shared["router_w"] = np.ascontiguousarray(
        np.asarray(inputs["L0_ffn_router_w"], np.float32))
    shared["gating_w"] = np.ascontiguousarray(
        np.asarray(inputs["L0_ffn_gating_w"], np.float32))
    shared["linear_w"] = np.ascontiguousarray(
        np.asarray(inputs["L0_ffn_linear_w"], np.float32))
    shared.update(_consts(inputs))
    return [dict(shared, x=x[b]) for b in range(8)]


def kernel(**inputs):
    nc = build_program()
    in_maps = _in_maps(inputs)
    res = run_bass_kernel_spmd(nc, in_maps, core_ids=list(range(8)))
    return np.stack([np.asarray(r["out"], np.float32) for r in res.results])


# version 18
# speedup vs baseline: 1.1052x; 1.1052x over previous
import contextlib
import numpy as np
import ml_dtypes
import concourse.bass as bass
import concourse.mybir as mybir
from concourse.bass_utils import run_bass_kernel_spmd

F32 = mybir.dt.float32
BF16 = mybir.dt.bfloat16
FP16 = mybir.dt.float16
U8 = mybir.dt.uint8
AF = mybir.ActivationFunctionType
ALU = mybir.AluOpType
AX = mybir.AxisListType

T = 8192
D = 1024
H = 16
DK = 64
LORA = 64
FF = 4096
E = 8
HID = 2048
TB = 512
NTB = T // TB
NFB = D // 128
C = 32
NCH = T // C

EPS_LN = 1e-5
EPS_RMS = 1e-6
EPS_GN = 1e-5
CLAMP = 42.0


class Task:
    __slots__ = ("tid", "eng", "fn", "inc", "deps", "cum", "drain", "hard")

    def __init__(self, tid, eng, fn, inc, deps):
        self.tid, self.eng, self.fn, self.inc, self.deps = tid, eng, fn, inc, deps
        self.cum = 0
        self.drain = False
        self.hard = ()


class Sched:

    ENGS = ("t", "v", "a", "g", "q")

    def __init__(self):
        self.tasks = []
        self.per_eng = {e: [] for e in self.ENGS}

    def add(self, eng, fn, deps=(), dma=0, drain=False, hard=()):
        deps = [d for d in deps if d is not None]
        inc = 16 * dma if dma else 1
        t = Task(len(self.tasks), eng, fn, inc, deps)
        t.drain = drain
        t.hard = [d for d in hard if d is not None]
        for d in deps:
            assert d.tid < t.tid, "dep must precede task"
        self.tasks.append(t)
        self.per_eng[eng].append(t)
        t.cum = (self.per_eng[eng][-2].cum if len(self.per_eng[eng]) > 1 else 0) + inc
        return t

    def lasts(self):
        return [self.per_eng[e][-1] for e in self.ENGS if self.per_eng[e]]

    def emit(self, nc, block, sems):
        eng_objs = {"t": nc.tensor, "v": nc.vector, "a": nc.scalar,
                    "g": nc.gpsimd, "q": nc.sync}
        finals = {e: (self.per_eng[e][-1].cum if self.per_eng[e] else 0)
                  for e in self.ENGS}

        def mk(eng):
            def emit_stream(inner):
                watermark = {e: 0 for e in self.ENGS}
                for t in self.per_eng[eng]:
                    need = {}
                    for d in t.deps:
                        if d.eng == eng:
                            continue
                        need[d.eng] = max(need.get(d.eng, 0), d.cum)
                    for d in t.hard:
                        need[d.eng] = max(need.get(d.eng, 0), d.cum)
                    for pe in self.ENGS:
                        if pe in need and need[pe] > watermark[pe]:
                            inner.wait_ge(sems[pe], need[pe])
                            watermark[pe] = need[pe]
                    r = t.fn(inner)
                    if t.drain:
                        inner.drain()
                        r = None if r is None else r
                        inner.nop().then_inc(sems[eng], t.inc)
                        continue
                    if t.inc >= 16:
                        assert isinstance(r, (list, tuple)) and \
                            len(r) * 16 == t.inc, f"dma task {t.tid} count"
                        for inst in r:
                            inst.then_inc(sems[eng], 16)
                    elif r is None:
                        inner.nop().then_inc(sems[eng], 1)
                    else:
                        r.then_inc(sems[eng], 1)
                if eng == "q":
                    for pe in self.ENGS:
                        if pe != "q" and finals[pe] > watermark[pe]:
                            inner.wait_ge(sems[pe], finals[pe])
                    inner.wait_ge(sems["q"], finals["q"])
            return emit_stream

        block.tensor(mk("t"))
        block.vector(mk("v"))
        block.scalar(mk("a"))
        block.gpsimd(mk("g"))
        block.sync(mk("q"))


class Res:

    def __init__(self, name=""):
        self.name, self.writer, self.readers = name, None, []

    def rdeps(self):
        return [self.writer] if self.writer else []

    def wdeps(self):
        return ([self.writer] if self.writer else []) + self.readers

    def did_read(self, task):
        self.readers.append(task)
        return task

    def did_write(self, task):
        self.writer, self.readers = task, []
        return task


def build_program(debug_taps=(), dbg_raw_o=False, skip=()):
    nc = bass.Bass("TRN2", target_bir_lowering=False, debug=False)

    def inp(name, shape, dtype=F32):
        return nc.dram_tensor(name, list(shape), dtype, kind="ExternalInput").ap()

    x_in = inp("x", (T, D))
    Wr = inp("Wr", (D, D)); Wk = inp("Wk", (D, D)); Wv = inp("Wv", (D, D))
    Wg = inp("Wg", (D, D)); Wo = inp("Wo", (D, D))
    w0 = inp("w0", (D,))
    wA = inp("wA", (D, LORA)); wB = inp("wB", (LORA, D))
    ffn_w1 = inp("ffn_w1", (D, FF)); ffn_w2 = inp("ffn_w2", (FF, D))
    router_w = inp("router_w", (D, E))
    gating_w = inp("gating_w", (E, 2, HID, D))
    linear_w = inp("linear_w", (E, HID, D))
    vec_cols = inp("vec_cols", (128, 8 * NFB))
    m_low_i = inp("m_low", (128, 512), U8)
    m_hi_i = inp("m_hi", (128, 512), U8)
    m_D1_i = inp("m_D1", (128, 512), U8)
    m_D2_i = inp("m_D2", (128, 512), U8)
    gnw_rep = inp("gnw_rep", (128, D))
    gnb_rep = inp("gnb_rep", (128, D))
    ln2w_rep = inp("ln2w_rep", (128, D))
    pes_rep = inp("pes_rep", (128, E))
    ident_in = inp("ident", (128, 128))

    out_t = nc.dram_tensor("out", [T, D], F32, kind="ExternalOutput").ap()

    def plane(name, shape, dtype=BF16):
        return nc.dram_tensor(name, list(shape), dtype, kind="Internal").ap()

    P_xn = plane("P_xn", (T, D), FP16)
    P_u = plane("P_u", (T, D), FP16)
    P_v = plane("P_v", (T, D), FP16)
    P_g = plane("P_g", (T, D), FP16)
    P_rt16 = plane("P_rt16", (D, T))
    P_kt16 = plane("P_kt16", (D, T))
    P_rc16 = plane("P_rc16", (D, T), FP16)
    P_kc16 = plane("P_kc16", (D, T), FP16)
    P_ro32 = plane("P_ro32", (D, T), FP16)
    P_ks32 = plane("P_ks32", (D, T), FP16)
    P_ku = plane("P_ku", (D, T), FP16)
    P_rr = plane("P_rr", (D, T), FP16)
    P_dec = plane("P_dec", (D, NCH), F32)
    P_ct = plane("P_ct", (T, E), F32)
    P_o = plane("P_o", (T, D), FP16)
    P_xa = plane("P_xa", (T, D), F32)
    P_h2 = plane("P_h2", (T, D), FP16)
    P_s = plane("P_s", (T, D), F32)
    P_gw16 = plane("P_gw16", (E, 2, HID, D), FP16)
    P_gwT = plane("P_gwT", (E, 2, D, HID), FP16)
    P_m = plane("P_m", (T, D), FP16)
    P_odbg = plane("P_odbg", (T, D), F32) if dbg_raw_o else None
    P_adbg = plane("P_adbg", (HID, 512), F32) if dbg_raw_o else None

    planes = dict(P_xn=P_xn, P_u=P_u, P_v=P_v, P_g=P_g, P_rt16=P_rt16,
                  P_kt16=P_kt16, P_rc16=P_rc16, P_kc16=P_kc16, P_ro32=P_ro32,
                  P_ks32=P_ks32, P_ku=P_ku, P_rr=P_rr, P_dec=P_dec, P_ct=P_ct,
                  P_o=P_o, P_xa=P_xa, P_h2=P_h2, P_s=P_s, P_m=P_m,
                  P_gwT=P_gwT)
    if dbg_raw_o:
        planes["P_odbg"] = P_odbg
        planes["P_adbg"] = P_adbg
    taps = {n: nc.dram_tensor("tap_" + n, list(planes[n].shape),
                              planes[n].dtype, kind="ExternalOutput").ap()
            for n in debug_taps}

    sched = Sched()
    stack = contextlib.ExitStack()
    cur = [None]

    def sbuf(name, shape, dtype=F32):
        st = cur[0] if cur[0] is not None else stack
        return st.enter_context(nc.sbuf_tensor(name, list(shape), dtype))

    def phase_barrier():
        for e in Sched.ENGS:
            sched.add(e, lambda x: None,
                      deps=[t for t in sched.lasts()])

    with stack:
        sems = {e: stack.enter_context(nc.semaphore("sem_" + e))
                for e in Sched.ENGS}
        PS = stack.enter_context(nc.psum_tensor("PS", [128, 4096], F32))

        def bank(i, cols=512):
            return PS[:, i * 512:i * 512 + cols]

        cvec = sbuf("cvec", (128, 8 * NFB), F32)
        c_pes = sbuf("c_pes", (128, E), F32)
        c_ones = sbuf("c_ones", (128, 512), BF16)
        c_eps = sbuf("c_eps", (128, 4), F32)

        def vcol(vi, fb):
            return cvec[:, vi * NFB + fb:vi * NFB + fb + 1]
        V_LN1, V_MUR, V_MUK, V_MUV, V_MUW, V_MUG, V_U, V_RSC = range(8)

        t_const = sched.add("q", lambda e: [
            e.dma_start(out=cvec[:], in_=vec_cols),
            e.dma_start(out=c_pes[:], in_=pes_rep),
        ], dma=2)
        def f_memc(e):
            e.memset(c_ones[:], 1.0)
            e.memset(c_eps[:, 0:1], EPS_LN)
            e.memset(c_eps[:, 1:2], EPS_RMS)
            return e.memset(c_eps[:, 2:3], EPS_GN)
        t_memc = sched.add("v", f_memc)

        ph01 = contextlib.ExitStack()
        cur[0] = ph01
        t_gwc = sched.add("g", lambda e: [
            e.dma_start(out=P_gw16[e_, g_], in_=gating_w[e_, g_])
            for e_ in range(E) for g_ in range(2)], dma=16)
        gtt = [sbuf(f"gtt{i}", (128, HID), FP16) for i in range(2)]
        gtt_res = [Res(f"gtt{i}") for i in range(2)]
        gti = 0
        for e_ in range(E):
            for g_ in range(2):
                for dc in range(NFB):
                    gb_ = gti % 2
                    gti += 1
                    t_t1 = sched.add("q", (lambda e, e_=e_, g_=g_, dc=dc,
                                           gb_=gb_: [
                        e.dma_start(out=gtt[gb_][:],
                                    in_=P_gw16[e_, g_][:, dc * 128:(dc + 1) * 128],
                                    transpose=True)]),
                        deps=[t_gwc] + gtt_res[gb_].wdeps(), dma=1)
                    gtt_res[gb_].did_write(t_t1)
                    t_t2 = sched.add("g", (lambda e, e_=e_, g_=g_, dc=dc,
                                           gb_=gb_: [
                        e.dma_start(out=P_gwT[e_, g_, dc * 128:(dc + 1) * 128, :],
                                    in_=gtt[gb_][:])]), deps=[t_t1], dma=1)
                    gtt_res[gb_].did_read(t_t2)


        _p0x = sbuf("p0_x", (128, D), F32)
        _p0xn = sbuf("p0_xn", (128, D), F32)
        p0_x = [_p0x, _p0x]
        p0_xn = [_p0xn, _p0xn]
        p0_xnb = [sbuf(f"p0_xnb{i}", (128, D), FP16) for i in range(2)]
        p0_ub = [sbuf(f"p0_ub{i}", (128, D), FP16) for i in range(2)]
        p0_stat = [sbuf(f"p0_st{i}", (128, 8), F32) for i in range(2)]
        _p0xr = Res()
        p0_x_res = [_p0xr, _p0xr]
        p0_o_res = [Res() for _ in range(2)]
        p0_s_res = [Res() for _ in range(2)]
        sq_scr_t = sbuf("sq_scr", (128, D), F32)
        sq_scr = sq_scr_t[:]

        p0_done = []
        for i in range(T // 128):
            b = i % 2
            xb, xnb, xnbb, ub, st = (p0_x[b], p0_xn[b], p0_xnb[b], p0_ub[b],
                                     p0_stat[b])
            xr_, or_, sr_ = p0_x_res[b], p0_o_res[b], p0_s_res[b]

            t_ld = sched.add("q", (lambda e, d=xb, s=x_in[i * 128:(i + 1) * 128, :]:
                                   [e.dma_start(out=d[:], in_=s)]),
                             deps=xr_.wdeps(), dma=1)
            xr_.did_write(t_ld)

            def f_mean(e, xb=xb, st=st):
                e.reduce_sum(st[:, 0:1], xb[:], axis=AX.X)
                e.drain()
                return e.tensor_scalar(st[:, 0:1], st[:, 0:1], 1.0 / D, None,
                                       ALU.mult)
            t_mean = sched.add("v", f_mean, drain=True,
                               deps=xr_.rdeps() + [t_memc] + sr_.wdeps())
            xr_.did_read(t_mean)
            sr_.did_write(t_mean)

            def f_sq(e, xb=xb, st=st):
                return e.activation(sq_scr, xb[:], AF.Square,
                                    accum_out=st[:, 2:3])
            t_sq = sched.add("a", f_sq, deps=xr_.rdeps() + [t_mean])
            xr_.did_read(t_sq)

            def f_var(e, st=st):
                e.tensor_scalar(st[:, 3:4], st[:, 2:3], 1.0 / D, None, ALU.mult)
                e.tensor_tensor(st[:, 4:5], st[:, 0:1], st[:, 0:1], ALU.mult)
                e.drain()
                return e.tensor_tensor(st[:, 3:4], st[:, 3:4], st[:, 4:5],
                                       ALU.subtract)
            t_var = sched.add("v", f_var, deps=[t_sq], drain=True)

            def f_rstd(e, st=st):
                e.activation(st[:, 1:2], st[:, 3:4], AF.Ln, bias=c_eps[:, 0:1])
                e.drain()
                return e.activation(st[:, 1:2], st[:, 1:2], AF.Exp, scale=-0.5)
            t_rstd = sched.add("a", f_rstd, deps=[t_var, t_memc], drain=True)

            def f_xn(e, xb=xb, xnb=xnb, st=st):
                return e.tensor_scalar(xnb[:], xb[:], st[:, 0:1], st[:, 1:2],
                                       ALU.subtract, ALU.mult)
            t_xn = sched.add("v", f_xn, deps=xr_.rdeps() + [t_rstd])
            xr_.did_read(t_xn)

            def f_ss2(e, xnb=xnb, st=st):
                return e.activation(sq_scr, xnb[:], AF.Square,
                                    accum_out=st[:, 5:6])
            t_ss2 = sched.add("a", f_ss2, deps=[t_xn])

            def f_s1(e, st=st):
                e.activation(st[:, 6:7], st[:, 5:6], AF.Ln, bias=c_eps[:, 1:2],
                             scale=1.0 / D)
                e.drain()
                return e.activation(st[:, 6:7], st[:, 6:7], AF.Exp, scale=-0.5)
            t_s1 = sched.add("a", f_s1, deps=[t_ss2, t_memc], drain=True)

            def f_fin(e, xnb=xnb, xnbb=xnbb, ub=ub, st=st):
                e.tensor_copy(xnbb[:], xnb[:])
                return e.tensor_scalar(ub[:], xnb[:], st[:, 6:7], None, ALU.mult)
            t_fin = sched.add("v", f_fin, deps=[t_s1] + or_.wdeps())
            or_.did_write(t_fin)
            sr_.did_read(t_fin)

            t_sp = sched.add("g", (lambda e, xnbb=xnbb, ub=ub, i=i: [
                e.dma_start(out=P_xn[i * 128:(i + 1) * 128, :], in_=xnbb[:]),
                e.dma_start(out=P_u[i * 128:(i + 1) * 128, :], in_=ub[:]),
            ]), deps=[t_fin], dma=2)
            or_.did_read(t_sp)
            p0_done.append(t_sp)

        bres = [Res(f"bank{i}") for i in range(8)]

        wr_s = sbuf("wr_s", (128, NFB, D), FP16)
        wk_s = sbuf("wk_s", (128, NFB, D), FP16)
        wv_s = sbuf("wv_s", (128, NFB, D), FP16)
        wg_s = sbuf("wg_s", (128, NFB, D), FP16)
        wa_s = sbuf("wa_s", (128, NFB, LORA), FP16)
        wbx_s = sbuf("wbx_s", (65, D), FP16)
        rw_s = sbuf("rw_s", (128, NFB, E), FP16)
        t_wld = sched.add("g", lambda e: [
            e.dma_start(out=wr_s[:], in_=Wr.rearrange("(b p) o -> p b o", p=128)),
            e.dma_start(out=wk_s[:], in_=Wk.rearrange("(b p) o -> p b o", p=128)),
            e.dma_start(out=wv_s[:], in_=Wv.rearrange("(b p) o -> p b o", p=128)),
            e.dma_start(out=wg_s[:], in_=Wg.rearrange("(b p) o -> p b o", p=128)),
            e.dma_start(out=wa_s[:], in_=wA.rearrange("(b p) o -> p b o", p=128)),
            e.dma_start(out=wbx_s[0:64, :], in_=wB),
            e.dma_start(out=wbx_s[64:65, :], in_=w0.unsqueeze(0)),
            e.dma_start(out=rw_s[:], in_=router_w.rearrange("(b p) o -> p b o", p=128)),
        ], dma=8)

        ufm = [sbuf(f"ufm{i}", (128, NFB, 512), FP16) for i in range(2)]
        ufm_res = [Res(f"ufm{i}") for i in range(2)]
        ucar = sbuf("ucar", (128, NFB), F32)
        ucar_res = Res("ucar")
        h_s = sbuf("h_s", (128, NFB, 513), FP16)
        h_res = Res("h")
        xx_s = sbuf("xx_s", (128, NFB, 512), FP16)
        xx_res = Res("xx")
        slab = [sbuf(f"xslab{i}", (128, NFB, 512), FP16) for i in range(2)]
        slab_res = [Res(f"slab{i}") for i in range(2)]

        e_s = sbuf("e_s", (128, 512), F32)
        es_res = Res("e_s")
        Ls = sbuf("Ls", (128, NFB, 513), F32)
        Ls_res = Res("Ls")
        t_ls0 = sched.add("v", lambda e: e.memset(Ls[:, :, 0:1], 0.0),
                          drain=True)
        Ls_res.did_write(t_ls0)
        th_s = sbuf("th_s", (65, 512), BF16)
        th_res = Res("th")
        argt = [sbuf(f"argt{i}", (128, 512), F32) for i in range(2)]
        arg_res = [Res(f"arg{i}") for i in range(2)]
        exf_t = sbuf("exf", (128, 512), BF16)
        exf = [exf_t, exf_t]
        exf_r1 = Res("exf")
        exf_res = [exf_r1, exf_r1]
        m16_t = sbuf("m16_t", (128, 32), F32)
        m16_res = Res("m16")
        dec_t = sbuf("dec_t", (128, 16), F32)
        decx = sbuf("decx", (128, 16), F32)
        dec_res = Res("dec")
        FPL = ("rt16", "rc16", "rr", "kt16", "kc16", "ks32", "ku", "ro32")
        kk_stg = sbuf("kk_stg", (128, 512), FP16)
        kk_res = Res("kk_stg")
        fstg = {n: sbuf("stg_" + n, (128, 512),
                        BF16 if n in ("rt16", "kt16") else FP16) for n in FPL}
        fstg_res = {n: Res("stg_" + n) for n in FPL}
        fplane = {"rt16": P_rt16, "rc16": P_rc16, "rr": P_rr, "kt16": P_kt16,
                  "kc16": P_kc16, "ks32": P_ks32, "ku": P_ku, "ro32": P_ro32}
        vsp = sbuf("vsp", (128, 4, D), FP16)
        vsp_res = Res("vsp")
        gsp = sbuf("gsp", (128, 4, D), FP16)
        gsp_res = Res("gsp")
        lgs = sbuf("lgs", (8, 512), F32)
        lgs_res = Res("lgs")
        ltm = sbuf("ltm", (128, 4, E), F32)
        rt_a = sbuf("rt_a", (128, 4, E), F32)
        rt_b = sbuf("rt_b", (128, 4, E), F32)
        rt_c = sbuf("rt_c", (128, 4, E), F32)
        rt_u = sbuf("rt_u", (128, 4, E), U8)
        rt_st = sbuf("rt_st", (128, 16), F32)
        rtr_res = Res("router")
        cfm_s = sbuf("cfm_s", (8, 512), F32)
        cfm_res = Res("cfm")
        router_prev = [None]
        c_ident = sbuf("c_ident", (128, 128), F32)
        c_neginf = sbuf("c_neginf", (128, E), F32)
        t_ident = sched.add("q", lambda e: [
            e.dma_start(out=c_ident[:], in_=ident_in)], dma=1)
        t_ninf = sched.add("v", lambda e: e.memset(c_neginf[:], -1e30))
        t_thones = sched.add("v", lambda e: e.memset(th_s[64:65, :], 1.0))
        th_res.did_write(t_thones)

        def bcast8(t2d):
            return t2d.unsqueeze(2).broadcast_to([128, 4, E])

        p1_last = []
        for tb in range(NTB):
            toff = 0
            buf = tb % 2
            uf, ures = ufm[buf], ufm_res[buf]
            if tb > 0:
                pu, pres = ufm[(tb - 1) % 2], ufm_res[(tb - 1) % 2]

                def f_ucar(e, pu=pu):
                    last = None
                    for fb in range(NFB):
                        last = e.tensor_scalar(
                            ucar[:, fb:fb + 1], pu[:, fb, 511:512],
                            vcol(V_LN1, fb), None, ALU.mult)
                    return last
                t_uc = sched.add("v", f_ucar,
                                 deps=pres.rdeps() + [t_const]
                                 + ucar_res.wdeps(), drain=True)
                pres.did_read(t_uc)
                ucar_res.did_write(t_uc)
            t_ul = sched.add("q", (lambda e, uf=uf, tb=tb: [
                e.dma_start(
                    out=uf[:, fb, :],
                    in_=P_u[tb * 512:(tb + 1) * 512,
                            fb * 128:(fb + 1) * 128],
                    transpose=True)
                for fb in range(NFB)]),
                deps=ures.wdeps() + p0_done[tb * 4:tb * 4 + 4],
                dma=NFB)
            ures.did_write(t_ul)

            def f_h(e, uf=uf, tb=tb):
                last = None
                for fb in range(NFB):
                    last = e.tensor_scalar(
                        h_s[:, fb, 1:513], uf[:, fb, 0:512],
                        vcol(V_LN1, fb), None, ALU.mult)
                if tb == 0:
                    last = e.memset(h_s[:, :, 0:1], 0.0)
                else:
                    last = e.tensor_copy(h_s[:, :, 0:1],
                                         ucar[:].unsqueeze(2))
                return last
            hdeps = ures.rdeps() + [t_const] + h_res.wdeps()
            if tb > 0:
                hdeps += ucar_res.rdeps()
            t_h = sched.add("v", f_h, deps=hdeps, drain=True)
            ures.did_read(t_h)
            h_res.did_write(t_h)
            if tb > 0:
                ucar_res.did_read(t_h)

            def f_xx(e):
                return e.tensor_tensor(xx_s[:], h_s[:, :, 0:512],
                                       h_s[:, :, 1:513], ALU.subtract)
            t_xx = sched.add("v", f_xx, deps=[t_h] + xx_res.wdeps())
            h_res.did_read(t_xx)
            xx_res.did_write(t_xx)

            def mk_slab(vi, sbuf_i):
                sl, sres = slab[sbuf_i], slab_res[sbuf_i]

                def f_sl(e):
                    last = None
                    for fb in range(NFB):
                        last = e.scalar_tensor_tensor(
                            sl[:, fb, :], xx_s[:, fb, :], vcol(vi, fb),
                            h_s[:, fb, 1:513], ALU.mult, ALU.add)
                    return last
                t_sl = sched.add("v", f_sl,
                                 deps=[t_xx] + sres.wdeps() + h_res.rdeps()[:0])
                h_res.did_read(t_sl)
                xx_res.did_read(t_sl)
                sres.did_write(t_sl)
                return t_sl

            t_xw = mk_slab(V_MUW, 0)
            pw = PS[0:64, 0:512]

            def f_wa(e, sl=slab[0]):
                last = None
                for dinb in range(NFB):
                    last = e.matmul(pw, wa_s[:, dinb, :], sl[:, dinb, :],
                                    start=(dinb == 0), stop=(dinb == NFB - 1))
                return last
            t_wa = sched.add("t", f_wa,
                             deps=[t_xw, t_wld] + bres[0].wdeps())
            slab_res[0].did_read(t_wa)
            bres[0].did_write(t_wa)

            def f_tanh(e):
                return e.activation(th_s[0:64, :], pw, AF.Tanh)
            t_tanh = sched.add("a", f_tanh, deps=[t_wa] + th_res.wdeps()
                               + [t_thones])
            bres[0].did_read(t_tanh)
            th_res.did_write(t_tanh)
            scans = []
            for b in range(NFB):
                bk = 1 + (b % 2)

                def f_wb(e, b=b, bk=bk):
                    return e.matmul(bank(bk), wbx_s[:, b * 128:(b + 1) * 128],
                                    th_s[:], start=True, stop=True)
                t_wb = sched.add("t", f_wb, deps=[t_tanh] + bres[bk].wdeps())
                th_res.did_read(t_wb)
                bres[bk].did_write(t_wb)

                def f_e(e, bk=bk):
                    return e.activation(e_s[:], bank(bk), AF.Exp)
                t_e = sched.add("a", f_e, deps=[t_wb] + es_res.wdeps())
                bres[bk].did_read(t_e)
                es_res.did_write(t_e)

                def f_scan(e, b=b, tb=tb):
                    r = e.tensor_tensor_scan(
                        Ls[:, b, 1:513], c_ones[:, 0:512], e_s[:],
                        Ls[:, b, 0:1] if tb > 0 else 0.0,
                        ALU.mult, ALU.subtract)
                    return r
                t_sc = sched.add("v", f_scan,
                                 deps=[t_e, t_memc] + Ls_res.wdeps(),
                                 drain=True)
                es_res.did_read(t_sc)
                Ls_res.did_write(t_sc)
                scans.append(t_sc)
                Lb = Ls[:, b, :]

                def f_dec(e, Lb=Lb):
                    e.tensor_tensor(dec_t[:], Lb[:, 32:513:32],
                                    Lb[:, 0:481:32], ALU.subtract)
                    e.drain()
                    return None
                t_dec = sched.add("v", f_dec, deps=[t_sc] + dec_res.wdeps(),
                                  drain=True)
                Ls_res.did_read(t_dec)

                def f_decx(e):
                    return e.activation(decx[:], dec_t[:], AF.Exp)
                t_dx = sched.add("a", f_decx, deps=[t_dec])
                dec_res.did_write(t_dx)
                t_dsp = sched.add("g", (lambda e, b=b, tb=tb: [
                    e.dma_start(out=P_dec[b * 128:(b + 1) * 128,
                                          tb * 16:(tb + 1) * 16],
                                in_=decx[:])]), deps=[t_dx], dma=1)
                dec_res.did_read(t_dsp)

            def factor(name, b, t_mm, bk, kind, clamp, extra_dep=None,
                       read_res=None):
                ai = 0 if name in ("rt16", "kt16", "ks32") else 1
                at, ar = argt[ai], arg_res[ai]
                xt, xr2 = exf[ai], exf_res[ai]
                Lb = Ls[:, b, :]
                Lx = Lb[:, 0:512].rearrange("p (c t) -> p c t", c=kind[3])
                Lv = Lb[:, 1:513].rearrange("p (c t) -> p c t", c=kind[3])

                def f_arg(e):
                    in0 = kind[0](Lb)
                    base = Lx if kind[2] == "lx" else Lv
                    r = e.scalar_tensor_tensor(
                        at[:].rearrange("p (c t) -> p c t", c=kind[3]),
                        in0, kind[1], base, ALU.mult,
                        ALU.add if kind[2] == "lx" else ALU.subtract)
                    if clamp:
                        r = e.tensor_scalar(at[:], at[:], CLAMP, None, ALU.min)
                    return r
                deps = [scans[b]] + ar.wdeps()
                if extra_dep is not None:
                    deps.append(extra_dep)
                t_a = sched.add("v", f_arg, deps=deps)
                Ls_res.did_read(t_a)
                if read_res is not None:
                    read_res.did_read(t_a)
                ar.did_write(t_a)

                def f_exp(e):
                    return e.activation(xt[:], at[:], AF.Exp)
                t_x = sched.add("a", f_exp, deps=[t_a] + xr2.wdeps())
                ar.did_read(t_x)
                xr2.did_write(t_x)

                stg, sres = fstg[name], fstg_res[name]
                src = factor.src

                def f_mul(e, src=src):
                    if src is None:
                        return e.tensor_tensor(stg[:], bank(bk), xt[:], ALU.mult)
                    return e.tensor_tensor(stg[:], src[:], xt[:], ALU.mult)
                t_m = sched.add("v", f_mul,
                                deps=[t_x, t_mm] + sres.wdeps()
                                + ([factor.src_task] if src is not None else []))
                xr2.did_read(t_m)
                if src is None:
                    bres[bk].did_read(t_m)
                else:
                    factor.src_res.did_read(t_m)
                sres.did_write(t_m)
                t_s = sched.add("g", (lambda e, name=name, b=b, tb=tb: [
                    e.dma_start(out=fplane[name][b * 128:(b + 1) * 128,
                                                 tb * 512:(tb + 1) * 512],
                                in_=stg[:])]), deps=[t_m], dma=1)
                sres.did_read(t_s)
                return t_s
            factor.src = None
            factor.src_task = None
            factor.src_res = None

            def g_end16(Lb):
                return Lb[:, 16:513:16].unsqueeze(2).broadcast_to([128, 32, 16])

            def g_start16(Lb):
                return Lb[:, 0:497:16].unsqueeze(2).broadcast_to([128, 32, 16])

            def g_end32(Lb):
                return Lb[:, 32:513:32].unsqueeze(2).broadcast_to([128, 16, 32])

            def g_start32(Lb):
                return Lb[:, 0:481:32].unsqueeze(2).broadcast_to([128, 16, 32])

            def g_m16(Lb):
                return m16_t[:].unsqueeze(2).broadcast_to([128, 32, 16])

            t_xr = mk_slab(V_MUR, 1)
            for b in range(NFB):
                bk = 3 + (b % 2)

                def f_mm(e, b=b, bk=bk, sl=slab[1]):
                    last = None
                    for dinb in range(NFB):
                        last = e.matmul(bank(bk),
                                        wr_s[:, dinb, b * 128:(b + 1) * 128],
                                        sl[:, dinb, :],
                                        start=(dinb == 0), stop=(dinb == NFB - 1))
                    return last
                t_mm = sched.add("t", f_mm, deps=[t_xr] + bres[bk].wdeps())
                slab_res[1].did_read(t_mm)
                bres[bk].did_write(t_mm)

                def f_m16(e, b=b):
                    e.tensor_tensor(m16_t[:], Ls[:, b, 16:513:16],
                                    Ls[:, b, 0:497:16], ALU.add)
                    e.drain()
                    return None
                t_m16 = sched.add("v", f_m16,
                                  deps=[scans[b]] + m16_res.wdeps(),
                                  drain=True)
                Ls_res.did_read(t_m16)
                m16_res.did_write(t_m16)

                stg, sres = fstg["rr"], fstg_res["rr"]

                def f_rr(e, bk=bk, stg=stg):
                    return e.tensor_copy(stg[:], bank(bk))
                t_rr = sched.add("v", f_rr, deps=[t_mm] + sres.wdeps())
                bres[bk].did_read(t_rr)
                sres.did_write(t_rr)
                factor.src, factor.src_task, factor.src_res = stg, t_rr, sres
                factor("rt16", b, t_mm, bk, (g_m16, -0.5, "lx", 32), True,
                       extra_dep=t_m16, read_res=m16_res)
                factor("rc16", b, t_mm, bk, (g_start16, -1.0, "lx", 32), False)
                factor("ro32", b, t_mm, bk, (g_start32, -1.0, "lx", 16), False)
                factor.src = None
                t_rrs = sched.add("g", (lambda e, b=b, tb=tb, stg=stg: [
                    e.dma_start(out=P_rr[b * 128:(b + 1) * 128,
                                         tb * 512:(tb + 1) * 512],
                                in_=stg[:])]), deps=[t_rr], dma=1)
                sres.did_read(t_rrs)

            t_xk = mk_slab(V_MUK, 0)
            for b in range(NFB):
                bk = 3 + (b % 2)

                def f_mm(e, b=b, bk=bk, sl=slab[0]):
                    last = None
                    for dinb in range(NFB):
                        last = e.matmul(bank(bk),
                                        wk_s[:, dinb, b * 128:(b + 1) * 128],
                                        sl[:, dinb, :],
                                        start=(dinb == 0), stop=(dinb == NFB - 1))
                    return last
                t_mm = sched.add("t", f_mm, deps=[t_xk] + bres[bk].wdeps())
                slab_res[0].did_read(t_mm)
                bres[bk].did_write(t_mm)

                def f_m16(e, b=b):
                    e.tensor_tensor(m16_t[:], Ls[:, b, 16:513:16],
                                    Ls[:, b, 0:497:16], ALU.add)
                    e.drain()
                    return None
                t_m16 = sched.add("v", f_m16,
                                  deps=[scans[b]] + m16_res.wdeps(),
                                  drain=True)
                Ls_res.did_read(t_m16)
                m16_res.did_write(t_m16)

                kstg, ksres = kk_stg, kk_res
                def f_kk(e, bk=bk):
                    return e.tensor_copy(kstg[:], bank(bk))
                t_kk = sched.add("v", f_kk, deps=[t_mm] + ksres.wdeps())
                bres[bk].did_read(t_kk)
                ksres.did_write(t_kk)
                factor.src, factor.src_task, factor.src_res = kstg, t_kk, ksres
                factor("kt16", b, t_mm, bk, (g_m16, 0.5, "lv", 32), True,
                       extra_dep=t_m16, read_res=m16_res)
                factor("kc16", b, t_mm, bk, (g_end16, 1.0, "lv", 32), False)
                factor("ks32", b, t_mm, bk, (g_end32, 1.0, "lv", 16), False)
                factor.src = None
                stg, sres = fstg["ku"], fstg_res["ku"]

                def f_ku(e, b=b, stg=stg):
                    return e.tensor_scalar(stg[:], kstg[:], vcol(V_U, b),
                                           None, ALU.mult)
                t_ku = sched.add("v", f_ku, deps=[t_kk] + sres.wdeps())
                ksres.did_read(t_ku)
                sres.did_write(t_ku)
                t_kus = sched.add("g", (lambda e, b=b, tb=tb, stg=stg: [
                    e.dma_start(out=P_ku[b * 128:(b + 1) * 128,
                                         tb * 512:(tb + 1) * 512],
                                in_=stg[:])]), deps=[t_ku], dma=1)
                sres.did_read(t_kus)

            t_xv = mk_slab(V_MUV, 1)
            v_ev = []
            prev_vsp = vsp_res.writer
            for sub in range(4):
                for dh in range(2):
                    bk = 5 + ((sub * 2 + dh) % 2)

                    def f_mm(e, sub=sub, dh=dh, bk=bk, sl=slab[1]):
                        last = None
                        for dinb in range(NFB):
                            last = e.matmul(
                                bank(bk),
                                sl[:, dinb, sub * 128:(sub + 1) * 128],
                                wv_s[:, dinb, dh * 512:(dh + 1) * 512],
                                start=(dinb == 0), stop=(dinb == NFB - 1))
                        return last
                    t_mm = sched.add("t", f_mm, deps=[t_xv] + bres[bk].wdeps())
                    slab_res[1].did_read(t_mm)
                    bres[bk].did_write(t_mm)

                    def f_ev(e, sub=sub, dh=dh, bk=bk):
                        return e.tensor_copy(
                            vsp[:, sub, dh * 512:(dh + 1) * 512], bank(bk))
                    t_ev = sched.add("v", f_ev,
                                     deps=[t_mm] + ([prev_vsp] if prev_vsp else []))
                    bres[bk].did_read(t_ev)
                    v_ev.append(t_ev)
            t_vsp = sched.add("g", (lambda e, tb=tb: [
                e.dma_start(
                    out=P_v[tb * 512:(tb + 1) * 512, :]
                    .rearrange("(s p) d -> p s d", p=128),
                    in_=vsp[:])]), deps=v_ev, dma=1)
            vsp_res.did_write(t_vsp)

            t_xg = mk_slab(V_MUG, 0)
            g_ev = []
            prev_gsp = gsp_res.writer
            for sub in range(4):
                for dh in range(2):
                    bk = 5 + ((sub * 2 + dh) % 2)

                    def f_mm(e, sub=sub, dh=dh, bk=bk, sl=slab[0]):
                        last = None
                        for dinb in range(NFB):
                            last = e.matmul(
                                bank(bk),
                                sl[:, dinb, sub * 128:(sub + 1) * 128],
                                wg_s[:, dinb, dh * 512:(dh + 1) * 512],
                                start=(dinb == 0), stop=(dinb == NFB - 1))
                        return last
                    t_mm = sched.add("t", f_mm, deps=[t_xg] + bres[bk].wdeps())
                    slab_res[0].did_read(t_mm)
                    bres[bk].did_write(t_mm)

                    def f_ev(e, sub=sub, dh=dh, bk=bk):
                        return e.activation(
                            gsp[:, sub, dh * 512:(dh + 1) * 512], bank(bk),
                            AF.Silu)
                    t_ev = sched.add("a", f_ev,
                                     deps=[t_mm] + ([prev_gsp] if prev_gsp else []))
                    bres[bk].did_read(t_ev)
                    g_ev.append(t_ev)
            t_gsp = sched.add("g", (lambda e, tb=tb: [
                e.dma_start(
                    out=P_g[tb * 512:(tb + 1) * 512, :]
                    .rearrange("(s p) d -> p s d", p=128),
                    in_=gsp[:])]), deps=g_ev, dma=1)
            gsp_res.did_write(t_gsp)

            rin_s, rin_res = slab[1], slab_res[1]

            def f_rin(e, uf=uf, rin_s=rin_s):
                last = None
                for fb in range(NFB):
                    last = e.tensor_scalar(
                        rin_s[:, fb, :], uf[:, fb, :],
                        vcol(V_RSC, fb), 0.03125, ALU.mult, ALU.mult)
                return last
            t_rin = sched.add("v", f_rin,
                              deps=ures.rdeps() + [t_const] + rin_res.wdeps())
            ures.did_read(t_rin)
            rin_res.did_write(t_rin)

            def f_rmm(e):
                last = None
                for dinb in range(NFB):
                    last = e.matmul(PS[0:8, 7 * 512:8 * 512],
                                    rw_s[:, dinb, :], rin_s[:, dinb, :],
                                    start=(dinb == 0), stop=(dinb == NFB - 1))
                return last
            t_rmm = sched.add("t", f_rmm, deps=[t_rin, t_wld] + bres[7].wdeps())
            rin_res.did_read(t_rmm)
            bres[7].did_write(t_rmm)

            def f_lev(e):
                return e.tensor_copy(lgs[:], PS[0:8, 7 * 512:8 * 512])
            t_lev = sched.add("v", f_lev, deps=[t_rmm] + lgs_res.wdeps()
                              + ([router_prev[0]] if router_prev[0] else []))
            bres[7].did_read(t_lev)
            lgs_res.did_write(t_lev)

            t_tr_ev = []
            for sub in range(4):
                def f_tr(e, sub=sub):
                    return e.transpose(PS[:, 0:8],
                                       lgs[0:8, sub * 128:(sub + 1) * 128],
                                       c_ident[0:8, 0:8])
                t_tr = sched.add("t", f_tr,
                                 deps=[t_lev, t_ident] + bres[0].wdeps())
                lgs_res.did_read(t_tr)
                bres[0].did_write(t_tr)

                def f_trev(e, sub=sub):
                    return e.tensor_copy(ltm[:, sub, :], PS[:, 0:8])
                t_te = sched.add("v", f_trev, deps=[t_tr], drain=True)
                bres[0].did_read(t_te)
                t_tr_ev.append(t_te)

            def f_router(e):
                e.reduce_max(rt_st[:, 0:4], ltm[:], axis=AX.X)
                e.drain()
                mx = rt_st[:, 0:4].unsqueeze(2).broadcast_to([128, 4, E])
                e.tensor_tensor(rt_a[:], ltm[:], mx, ALU.subtract)
                return None
            t_r1 = sched.add("v", f_router, deps=t_tr_ev, drain=True)

            def f_rexp(e):
                return e.activation(rt_b[:], rt_a[:], AF.Exp)
            t_r2 = sched.add("a", f_rexp, deps=[t_r1])

            def f_r3(e):
                e.reduce_sum(rt_st[:, 4:8], rt_b[:], axis=AX.X)
                e.drain()
                e.reciprocal(rt_st[:, 8:12], rt_st[:, 4:8])
                e.drain()
                sm = rt_st[:, 8:12].unsqueeze(2).broadcast_to([128, 4, E])
                e.tensor_tensor(rt_b[:], rt_b[:], sm, ALU.mult)
                mx = rt_st[:, 0:4].unsqueeze(2).broadcast_to([128, 4, E])
                e.tensor_tensor(rt_a[:], ltm[:], mx, ALU.is_ge)
                e.drain()
                e.tensor_copy(rt_u[:], rt_a[:])
                e.drain()
                nf = c_neginf[:].unsqueeze(1).broadcast_to([128, 4, E])
                e.select(rt_c[:], rt_u[:], nf, ltm[:])
                e.drain()
                e.reduce_max(rt_st[:, 12:16], rt_c[:], axis=AX.X)
                e.drain()
                m2 = rt_st[:, 12:16].unsqueeze(2).broadcast_to([128, 4, E])
                e.tensor_tensor(rt_c[:], rt_c[:], m2, ALU.is_ge)
                e.drain()
                e.tensor_tensor(rt_a[:], rt_a[:], rt_c[:], ALU.add)
                e.drain()
                e.tensor_tensor(rt_a[:], rt_a[:], rt_b[:], ALU.mult)
                e.drain()
                e.reduce_sum(rt_st[:, 4:8], rt_a[:], axis=AX.X)
                e.drain()
                e.reciprocal(rt_st[:, 8:12], rt_st[:, 4:8])
                e.drain()
                rn = rt_st[:, 8:12].unsqueeze(2).broadcast_to([128, 4, E])
                e.tensor_tensor(rt_a[:], rt_a[:], rn, ALU.mult)
                e.drain()
                pes = c_pes[:, :].unsqueeze(1).broadcast_to([128, 4, E])
                return e.tensor_tensor(rt_a[:], rt_a[:], pes, ALU.mult)
            t_r3 = sched.add("v", f_r3, deps=[t_r2, t_ninf, t_const],
                             drain=True)

            t_csp = sched.add("g", (lambda e, tb=tb: [
                e.dma_start(out=P_ct[tb * 512:(tb + 1) * 512, :]
                            .rearrange("(s p) e -> p s e", p=128),
                            in_=rt_a[:])]), deps=[t_r3], dma=1)
            router_prev[0] = t_csp
            cfm_res.did_write(t_csp)
            cfm_res.did_read(t_csp)

            def f_lcar(e):
                return e.tensor_copy(Ls[:, :, 0:1], Ls[:, :, 512:513])
            t_lc = sched.add("v", f_lcar, deps=Ls_res.wdeps(), drain=True)
            Ls_res.did_write(t_lc)
            p1_last.append(t_csp)

        ph01.close()
        cur[0] = None
        phase_barrier()

        ph2 = contextlib.ExitStack()
        cur[0] = ph2
        c_mlow = sbuf("c_mlow", (128, 512), U8)
        c_mhi = sbuf("c_mhi", (128, 512), U8)
        c_mD1 = sbuf("c_mD1", (128, 512), U8)
        c_mD2 = sbuf("c_mD2", (128, 512), U8)
        c_gnw = sbuf("c_gnw", (128, D), BF16)
        c_gnb = sbuf("c_gnb", (128, D), BF16)
        t_const2 = sched.add("g", lambda e: [
            e.dma_start(out=c_mlow[:], in_=m_low_i),
            e.dma_start(out=c_mhi[:], in_=m_hi_i),
            e.dma_start(out=c_mD1[:], in_=m_D1_i),
            e.dma_start(out=c_mD2[:], in_=m_D2_i),
            e.dma_start(out=c_gnw[:], in_=gnw_rep),
            e.dma_start(out=c_gnb[:], in_=gnb_rep),
        ], dma=6)
        PB = 256
        NPB = T // PB
        SL = ("rt16", "kt16", "rc16", "kc16", "ro32", "ku", "rr")
        sl_t = {n: [sbuf(f"p2_{n}{i}", (128, NFB, PB),
                         BF16 if n in ("rt16", "kt16") else FP16)
                    for i in range(2)] for n in SL}
        sl_res = {n: [Res(f"p2_{n}{i}") for i in range(2)] for n in SL}
        vt_t = [sbuf(f"p2_v{i}", (128, 2, D), FP16) for i in range(2)]
        vt_res = [Res(f"p2_v{i}") for i in range(2)]
        vs_t = [sbuf(f"p2_vs{i}", (128, 2, D), FP16) for i in range(2)]
        vs_res = [Res(f"p2_vs{i}") for i in range(2)]
        gt_t = [sbuf(f"p2_g{i}", (128, 2, D), FP16) for i in range(2)]
        gt_res = [Res(f"p2_g{i}") for i in range(2)]
        ks_t = [sbuf(f"p2_ks{i}", (128, 2, D), FP16) for i in range(2)]
        ks_res = [Res(f"p2_ks{i}") for i in range(2)]
        dc_t = [sbuf(f"p2_dc{i}", (128, NFB, PB // C), F32) for i in range(2)]
        dc_res = [Res(f"p2_dc{i}") for i in range(2)]
        S32 = sbuf("S32", (128, 512), F32)
        S16 = sbuf("S16", (128, 512), FP16)
        S_res = Res("S")
        t_s0 = sched.add("v", lambda e: (e.memset(S32[:], 0.0),
                                         e.memset(S16[:], 0.0))[-1],
                         drain=True)
        S_res.did_write(t_s0)
        A1_s = sbuf("A1_s", (128, 512), BF16)
        A2_s = sbuf("A2_s", (128, 512), BF16)
        A_res = Res("A")
        c_zero16 = sbuf("c_zero16", (128, 512), BF16)
        t_z16 = sched.add("v", lambda e: e.memset(c_zero16[:], 0.0))
        o32 = sbuf("o32", (128, D), F32)
        osq = sbuf("osq", (128, D), F32)
        ost = sbuf("ost", (128, 64), F32)
        ogt = [sbuf(f"ogt{i}", (128, 2, D), FP16) for i in range(2)]
        ogt_res = [Res(f"ogt{i}") for i in range(2)]
        o_res = Res("o32")

        p2_done = []
        for pb in range(NPB):
            bi = pb % 2
            SLA = ("rt16", "kt16", "rc16", "kc16", "ku", "rr")
            def f_ld1a(e, pb=pb, bi=bi):
                return [e.dma_start(
                    out=sl_t[n][bi][:],
                    in_=fplane[n].rearrange("(b p) t -> p b t", p=128)
                    [:, :, pb * PB:(pb + 1) * PB]) for n in SLA]
            dep_lda = [r for n in SLA for r in sl_res[n][bi].wdeps()]
            t_ld1a = sched.add("q", f_ld1a, deps=dep_lda, dma=6)
            for n in SLA:
                sl_res[n][bi].did_write(t_ld1a)

            def f_ld1b(e, pb=pb, bi=bi):
                return [
                    e.dma_start(
                        out=sl_t["ro32"][bi][:],
                        in_=fplane["ro32"].rearrange("(b p) t -> p b t", p=128)
                        [:, :, pb * PB:(pb + 1) * PB]),
                    e.dma_start(
                        out=dc_t[bi][:],
                        in_=P_dec.rearrange("(b p) c -> p b c", p=128)
                        [:, :, pb * (PB // C):(pb + 1) * (PB // C)]),
                ]

            def f_ld2(e, pb=pb, bi=bi):
                out = [
                    e.dma_start(out=vt_t[bi][:],
                                in_=P_v[pb * PB:(pb + 1) * PB, :]
                                .rearrange("(s p) d -> p s d", p=128)),
                    e.dma_start(out=gt_t[bi][:],
                                in_=P_g[pb * PB:(pb + 1) * PB, :]
                                .rearrange("(s p) d -> p s d", p=128)),
                ]
                for sv_ in range(2):
                    st_ = pb * PB + sv_ * 128 + 16
                    nr = min(128, T - st_)
                    out.append(e.dma_start(
                        out=vs_t[bi][0:nr, sv_, :],
                        in_=P_v[st_:st_ + nr, :]))
                return out
            t_ld2 = sched.add("g", f_ld2,
                              deps=vt_res[bi].wdeps() + gt_res[bi].wdeps()
                              + vs_res[bi].wdeps(), dma=4)
            vt_res[bi].did_write(t_ld2)
            gt_res[bi].did_write(t_ld2)
            vs_res[bi].did_write(t_ld2)

            t_ld1b = sched.add("g", f_ld1b,
                               deps=sl_res["ro32"][bi].wdeps()
                               + dc_res[bi].wdeps(), dma=2)
            sl_res["ro32"][bi].did_write(t_ld1b)
            dc_res[bi].did_write(t_ld1b)

            def f_ld3(e, pb=pb, bi=bi):
                return [e.dma_start(out=ks_t[bi][:, s, :],
                                    in_=P_ks32[:, pb * PB + s * 128:
                                               pb * PB + (s + 1) * 128],
                                    transpose=True)
                        for s in range(2)]
            t_ld3 = sched.add("q", f_ld3, deps=ks_res[bi].wdeps(), dma=2)
            ks_res[bi].did_write(t_ld3)

            for grp in range(2):
                gbase = pb * PB + grp * 128
                def f_ad(e, bi=bi, grp=grp, par=0):
                    last = None
                    for cl in range(4):
                        rb = 32 * cl
                        cc = grp * 128 + cl * 32
                        for h in range(par, H, 2):
                            fb, pbase = h // 2, (h % 2) * 64
                            kt = sl_t["kt16"][bi][pbase:pbase + 64, fb, :]
                            rt = sl_t["rt16"][bi][pbase:pbase + 64, fb, :]
                            kc = sl_t["kc16"][bi][pbase:pbase + 64, fb, :]
                            rc = sl_t["rc16"][bi][pbase:pbase + 64, fb, :]
                            ku_ = sl_t["ku"][bi][pbase:pbase + 64, fb, :]
                            rr_ = sl_t["rr"][bi][pbase:pbase + 64, fb, :]
                            tp = (pbase, rb)
                            hc = h * 32
                            e.matmul(PS[rb:rb + 16, 0 * 512 + hc:0 * 512 + hc + 16],
                                     kt[:, cc:cc + 16], rt[:, cc:cc + 16],
                                     start=True, stop=True, tile_position=tp,
                                     skip_group_check=True)
                            e.matmul(PS[rb:rb + 16, 1 * 512 + hc + 16:1 * 512 + hc + 32],
                                     kt[:, cc + 16:cc + 32], rt[:, cc + 16:cc + 32],
                                     start=True, stop=True, tile_position=tp,
                                     skip_group_check=True)
                            e.matmul(PS[rb:rb + 16, 0 * 512 + hc + 16:0 * 512 + hc + 32],
                                     kc[:, cc:cc + 16], rc[:, cc + 16:cc + 32],
                                     start=True, stop=True, tile_position=tp,
                                     skip_group_check=True)
                            e.matmul(PS[rb:rb + 16, 2 * 512 + hc:2 * 512 + hc + 16],
                                     ku_[:, cc:cc + 16], rr_[:, cc:cc + 16],
                                     start=True, stop=True, tile_position=tp,
                                     skip_group_check=True)
                            last = e.matmul(
                                PS[rb:rb + 16, 2 * 512 + hc + 16:2 * 512 + hc + 32],
                                ku_[:, cc + 16:cc + 32], rr_[:, cc + 16:cc + 32],
                                start=True, stop=True, tile_position=tp,
                                skip_group_check=True)
                    return last
                adeps = [t_ld1a] + bres[0].wdeps() + bres[1].wdeps() + bres[2].wdeps()
                t_ad0 = sched.add("t", (lambda e, bi=bi, grp=grp: f_ad(e, bi, grp, 0)),
                                  deps=adeps)
                t_ad1 = sched.add("t", (lambda e, bi=bi, grp=grp: f_ad(e, bi, grp, 1)),
                                  deps=[t_ld1a], hard=[t_ad0])
                for n in ("kt16", "rt16", "kc16", "rc16", "ku", "rr"):
                    sl_res[n][bi].did_read(t_ad0)
                    sl_res[n][bi].did_read(t_ad1)
                for b_ in (0, 1, 2):
                    bres[b_].did_write(t_ad1)

                def f_ev1(e):
                    e.select(A1_s[:], c_mlow[:], PS[:, 0:512], c_zero16[:])
                    e.drain()
                    return e.copy_predicated(A1_s[:], c_mD1[:],
                                             PS[:, 2 * 512:3 * 512])
                t_e1 = sched.add("v", f_ev1, deps=[t_ad1, t_z16, t_const2]
                                 + A_res.wdeps(), drain=True)

                def f_ev2(e):
                    e.select(A2_s[:], c_mhi[:], PS[:, 512:1024], c_zero16[:])
                    e.drain()
                    return e.copy_predicated(A2_s[:], c_mD2[:],
                                             PS[:, 2 * 512:3 * 512])
                t_e2 = sched.add("v", f_ev2, deps=[t_ad1, t_z16, t_const2]
                                 + A_res.wdeps(), drain=True)
                A_res.did_write(t_e2)
                for b_ in (0, 1, 2):
                    bres[b_].did_read(t_e1)
                    bres[b_].did_read(t_e2)

                for cl in range(4):
                    rb = 32 * cl
                    c_glob = (gbase // 32) + cl
                    sv = grp
                    prev_merge = S_res.writer

                    def f_av(e, bi=bi, rb=rb, sv=sv, cl=cl):
                        last = None
                        for h in range(H):
                            hc = h * 32
                            vlo = vt_t[bi][rb:rb + 32, sv,
                                           h * 64:(h + 1) * 64][0:16, :]
                            vhi = vs_t[bi][rb:rb + 16, sv, h * 64:(h + 1) * 64]
                            ob = 3 * 512 + (h % 2) * 512 + (h // 2) * 64
                            e.matmul(PS[rb:rb + 32, ob:ob + 64],
                                     A1_s[rb:rb + 16, hc:hc + 32], vlo,
                                     start=True, stop=False,
                                     tile_position=(rb, rb),
                                     skip_group_check=True)
                            last = e.matmul(
                                PS[rb:rb + 32, ob:ob + 64],
                                A2_s[rb:rb + 16, hc:hc + 32], vhi,
                                start=False, stop=True,
                                tile_position=(rb, rb),
                                skip_group_check=True)
                        return last
                    avdeps = [t_e2, t_ld2] + \
                        (bres[3].wdeps() + bres[4].wdeps() if cl == 0 else [])
                    if "av" in skip:
                        f_av = (lambda e: None)
                    t_av = sched.add("t", f_av, deps=avdeps)
                    A_res.did_read(t_av)
                    vt_res[bi].did_read(t_av)
                    vs_res[bi].did_read(t_av)
                    bres[3].did_write(t_av)
                    bres[4].did_write(t_av)

                    def f_oi(e, bi=bi, rb=rb, cl=cl, grp=grp):
                        last = None
                        for h in range(H):
                            pbase = (h % 2) * 64
                            fb = h // 2
                            cc = grp * 128 + cl * 32
                            ro = sl_t["ro32"][bi][pbase:pbase + 64, fb,
                                                  cc:cc + 32]
                            ob = 5 * 512 + (h % 2) * 512 + (h // 2) * 64
                            last = e.matmul(
                                PS[rb:rb + 32, ob:ob + 64], ro,
                                S16[pbase:pbase + 64,
                                    (h // 2) * 64:(h // 2) * 64 + 64],
                                start=True, stop=True,
                                tile_position=(pbase, rb),
                                skip_group_check=True)
                        return last
                    oideps = [t_ld1b] + S_res.rdeps() + \
                        (bres[5].wdeps() + bres[6].wdeps() if cl == 0 else [])
                    if "oi" in skip:
                        f_oi = (lambda e: None)
                    t_oi = sched.add("t", f_oi, deps=oideps, hard=[t_av])
                    sl_res["ro32"][bi].did_read(t_oi)
                    S_res.did_read(t_oi)
                    bres[5].did_write(t_oi)
                    bres[6].did_write(t_oi)

                    def f_su(e, bi=bi, rb=rb, sv=sv):
                        last = None
                        for h in range(H):
                            par = h % 2
                            hk = ks_t[bi][rb:rb + 32, sv,
                                          h * 64:(h + 1) * 64]
                            vv = vt_t[bi][rb:rb + 32, sv,
                                          h * 64:(h + 1) * 64]
                            last = e.matmul(
                                PS[par * 64:par * 64 + 64,
                                   7 * 512 + (h // 2) * 64:
                                   7 * 512 + (h // 2) * 64 + 64],
                                hk, vv, start=True, stop=True,
                                tile_position=(rb, par * 64),
                                skip_group_check=True)
                        return last
                    sudeps = [t_ld3, t_ld2] + bres[7].wdeps() + \
                        ([prev_merge] if prev_merge else [])
                    t_su = sched.add("t", f_su, deps=sudeps, hard=[t_oi])
                    ks_res[bi].did_read(t_su)
                    vt_res[bi].did_read(t_su)
                    bres[7].did_write(t_su)

                    def f_merge(e, bi=bi, c_glob=c_glob):
                        cloc = c_glob % (PB // C)
                        db = dc_t[bi][:, :, cloc].unsqueeze(2) \
                            .broadcast_to([128, NFB, 64])
                        e.tensor_tensor(
                            S32[:].rearrange("p (g v) -> p g v", g=NFB),
                            S32[:].rearrange("p (g v) -> p g v", g=NFB),
                            db, ALU.mult)
                        e.drain()
                        e.tensor_tensor(S32[:], S32[:], PS[:, 7 * 512:8 * 512],
                                        ALU.add)
                        e.drain()
                        return e.tensor_copy(S16[:], S32[:])
                    t_mg = sched.add("v", f_merge,
                                     deps=[t_su, t_ld1b] + S_res.wdeps(),
                                     drain=True)
                    dc_res[bi].did_read(t_mg)
                    bres[7].did_read(t_mg)
                    S_res.did_write(t_mg)

                t_last_av = t_av
                t_last_oi = t_oi

                def f_oev(e):
                    if "oi" in skip:
                        e.tensor_copy(o32[:, 0:512], PS[:, 3 * 512:4 * 512])
                        e.drain()
                        return e.tensor_copy(o32[:, 512:1024],
                                             PS[:, 4 * 512:5 * 512])
                    if "av" in skip:
                        e.tensor_copy(o32[:, 0:512], PS[:, 5 * 512:6 * 512])
                        e.drain()
                        return e.tensor_copy(o32[:, 512:1024],
                                             PS[:, 6 * 512:7 * 512])
                    e.tensor_copy(o32[:, 0:512], PS[:, 3 * 512:4 * 512])
                    e.tensor_copy(o32[:, 512:1024], PS[:, 4 * 512:5 * 512])
                    e.drain()
                    e.tensor_tensor(o32[:, 0:512], o32[:, 0:512],
                                    PS[:, 5 * 512:6 * 512], ALU.add)
                    e.drain()
                    return e.tensor_tensor(o32[:, 512:1024], o32[:, 512:1024],
                                           PS[:, 6 * 512:7 * 512], ALU.add)
                t_oev = sched.add("v", f_oev,
                                 deps=[t_last_av, t_last_oi] + o_res.wdeps(),
                                 drain=True)
                o_res.did_write(t_oev)
                for b_ in (3, 4, 5, 6):
                    bres[b_].did_read(t_oev)

                def f_gn1(e):
                    ov = o32[:].rearrange("p (s v) -> p s v", v=64)
                    e.reduce_sum(ost[:, 0:16], ov, axis=AX.X)
                    e.drain()
                    e.tensor_scalar(ost[:, 0:16], ost[:, 0:16], 1.0 / 64,
                                    None, ALU.mult)
                    e.drain()
                    e.tensor_tensor(osq[:], o32[:], o32[:], ALU.mult)
                    e.drain()
                    e.reduce_sum(ost[:, 16:32], osq[:].rearrange(
                        "p (s v) -> p s v", v=64), axis=AX.X)
                    e.drain()
                    e.tensor_scalar(ost[:, 16:32], ost[:, 16:32], 1.0 / 64,
                                    None, ALU.mult)
                    e.drain()
                    e.tensor_tensor(ost[:, 32:48], ost[:, 0:16], ost[:, 0:16],
                                    ALU.mult)
                    e.drain()
                    return e.tensor_tensor(ost[:, 16:32], ost[:, 16:32],
                                           ost[:, 32:48], ALU.subtract)
                if dbg_raw_o:
                    t_odbg = sched.add("g", (lambda e, gb2_=gbase: [
                        e.dma_start(out=P_odbg[gb2_:gb2_ + 128, :],
                                    in_=o32[:])]), deps=[t_oev], dma=1)
                    o_res.did_read(t_odbg)
                t_gn1 = sched.add("v", f_gn1, deps=[t_oev], drain=True)
                o_res.did_read(t_gn1)

                def f_gn2(e):
                    e.activation(ost[:, 48:64], ost[:, 16:32], AF.Ln,
                                 bias=c_eps[:, 2:3])
                    e.drain()
                    return e.activation(ost[:, 48:64], ost[:, 48:64], AF.Exp,
                                        scale=-0.5)
                t_gn2 = sched.add("a", f_gn2, deps=[t_gn1, t_memc], drain=True)

                def f_gn3(e, bi=bi, grp=grp, og=ogt[bi]):
                    ov = o32[:].rearrange("p (s v) -> p s v", v=64)
                    mb = ost[:, 0:16].unsqueeze(2).broadcast_to([128, 16, 64])
                    rb_ = ost[:, 48:64].unsqueeze(2).broadcast_to([128, 16, 64])
                    e.tensor_tensor(ov, ov, mb, ALU.subtract)
                    e.drain()
                    e.tensor_tensor(ov, ov, rb_, ALU.mult)
                    e.drain()
                    e.tensor_tensor(o32[:], o32[:], c_gnw[:], ALU.mult)
                    e.drain()
                    e.tensor_tensor(o32[:], o32[:], c_gnb[:], ALU.add)
                    e.drain()
                    gperm = gt_t[bi][:, grp, :].rearrange(
                        "p (hp pa v) -> p pa hp v", hp=8, pa=2, v=64)
                    ogperm = og[:, grp, :].rearrange(
                        "p (hp pa v) -> p pa hp v", hp=8, pa=2, v=64)
                    o4 = o32[:].rearrange("p (pa hp v) -> p pa hp v",
                                          pa=2, hp=8, v=64)
                    return e.tensor_tensor(ogperm, o4, gperm, ALU.mult)
                gn3deps = [t_gn2, t_const2]
                if grp == 0:
                    gn3deps += ogt_res[bi].wdeps()
                t_gn3 = sched.add("v", f_gn3, deps=gn3deps, drain=True)
                o_res.did_read(t_gn3)
                gt_res[bi].did_read(t_gn3)
                if grp == 1:
                    ogt_res[bi].did_write(t_gn3)

            def f_osp(e, pb=pb, bi=bi):
                return [e.dma_start(
                    out=P_o[pb * PB:(pb + 1) * PB, :]
                    .rearrange("(s p) d -> p s d", p=128), in_=ogt[bi][:])]
            t_osp = sched.add("g", f_osp, deps=[t_gn3], dma=1)
            ogt_res[bi].did_read(t_osp)
            p2_done.append(t_osp)

        ph2.close()
        cur[0] = None
        phase_barrier()

        ph3a = contextlib.ExitStack()
        cur[0] = ph3a
        c_ln2 = sbuf("c_ln2", (128, D), BF16)
        t_const3 = sched.add("g", lambda e: [
            e.dma_start(out=c_ln2[:], in_=ln2w_rep)], dma=1)
        wo_s = sbuf("wo_s", (128, NFB, D), FP16)
        t_wo = sched.add("g", lambda e: [
            e.dma_start(out=wo_s[:], in_=Wo.rearrange("(b p) o -> p b o", p=128))
        ], dma=1)
        ofm = [sbuf(f"ofm{i}", (128, NFB, 512), FP16) for i in range(2)]
        ofm_res = [Res(f"ofm{i}") for i in range(2)]
        xns = [sbuf(f"xns{i}", (128, 4, D), FP16) for i in range(2)]
        xns_res = [Res(f"xns{i}") for i in range(2)]
        xasl = sbuf("xasl", (128, 4, D), F32)
        xa_res = Res("xa")
        h2sl = sbuf("h2sl", (128, 4, D), FP16)
        h2_res = Res("h2")
        p3st = sbuf("p3st", (128, 8), F32)
        p3st_res = Res("p3st")

        for tb in range(NTB):
            bi = tb % 2
            t_ol = sched.add("q", (lambda e, tb=tb, bi=bi: [
                e.dma_start(out=ofm[bi][:, fb, :],
                            in_=P_o[tb * 512:(tb + 1) * 512,
                                    fb * 128:(fb + 1) * 128], transpose=True)
                for fb in range(NFB)]), deps=ofm_res[bi].wdeps(), dma=NFB)
            ofm_res[bi].did_write(t_ol)
            t_xl = sched.add("g", (lambda e, tb=tb, bi=bi: [
                e.dma_start(out=xns[bi][:],
                            in_=P_xn[tb * 512:(tb + 1) * 512, :]
                            .rearrange("(s p) d -> p s d", p=128))]),
                deps=xns_res[bi].wdeps(), dma=1)
            xns_res[bi].did_write(t_xl)

            for sub in range(4):
                for dh in range(2):
                    bk = (sub * 2 + dh) % 4

                    def f_mm(e, bi=bi, sub=sub, dh=dh, bk=bk):
                        last = None
                        for dinb in range(NFB):
                            last = e.matmul(
                                bank(bk),
                                ofm[bi][:, dinb, sub * 128:(sub + 1) * 128],
                                wo_s[:, dinb, dh * 512:(dh + 1) * 512],
                                start=(dinb == 0), stop=(dinb == NFB - 1))
                        return last
                    t_mm = sched.add("t", f_mm,
                                     deps=[t_ol, t_wo] + bres[bk].wdeps())
                    ofm_res[bi].did_read(t_mm)
                    bres[bk].did_write(t_mm)

                    def f_ev(e, bi=bi, sub=sub, dh=dh, bk=bk):
                        return e.tensor_tensor(
                            xasl[:, sub, dh * 512:(dh + 1) * 512], bank(bk),
                            xns[bi][:, sub, dh * 512:(dh + 1) * 512], ALU.add)
                    t_ev = sched.add("v", f_ev,
                                     deps=[t_mm, t_xl] + (xa_res.wdeps()
                                                          if sub == 0 and dh == 0
                                                          else []))
                    bres[bk].did_read(t_ev)
                    xns_res[bi].did_read(t_ev)
                    if sub == 3 and dh == 1:
                        xa_res.did_write(t_ev)
                    t_last_ev = t_ev

            t_xasp = sched.add("g", (lambda e, tb=tb: [
                e.dma_start(out=P_xa[tb * 512:(tb + 1) * 512, :]
                            .rearrange("(s p) d -> p s d", p=128),
                            in_=xasl[:])]), deps=[t_last_ev], dma=1)
            xa_res.did_read(t_xasp)

            for sub in range(4):
                def f_ss(e, sub=sub):
                    return e.activation(bank(4, 1024), xasl[:, sub, :],
                                        AF.Square, accum_out=p3st[:, 0:1])
                t_ss = sched.add("a", f_ss, deps=[t_last_ev] + p3st_res.wdeps())
                xa_res.did_read(t_ss)
                p3st_res.did_write(t_ss)
                bres[4].did_write(t_ss)
                bres[5].did_write(t_ss)

                def f_s2(e):
                    e.activation(p3st[:, 1:2], p3st[:, 0:1], AF.Ln,
                                 bias=c_eps[:, 1:2], scale=1.0 / D)
                    e.drain()
                    return e.activation(p3st[:, 1:2], p3st[:, 1:2], AF.Exp,
                                        scale=-0.5)
                t_s2 = sched.add("a", f_s2, deps=[t_ss, t_memc], drain=True)

                def f_h2(e, sub=sub):
                    e.tensor_scalar(h2sl[:, sub, :], xasl[:, sub, :],
                                    p3st[:, 1:2], None, ALU.mult)
                    e.drain()
                    return e.tensor_tensor(h2sl[:, sub, :], h2sl[:, sub, :],
                                           c_ln2[:], ALU.mult)
                t_h2 = sched.add("v", f_h2,
                                 deps=[t_s2, t_const3] + (h2_res.wdeps()
                                                          if sub == 0 else []),
                                 drain=True)
                xa_res.did_read(t_h2)
                p3st_res.did_read(t_h2)
                if sub == 3:
                    h2_res.did_write(t_h2)

            t_h2sp = sched.add("g", (lambda e, tb=tb: [
                e.dma_start(out=P_h2[tb * 512:(tb + 1) * 512, :]
                            .rearrange("(s p) d -> p s d", p=128),
                            in_=h2sl[:])]), deps=[t_h2], dma=1)
            h2_res.did_read(t_h2sp)
            p3a_last = t_h2sp

        ph3a.close()
        cur[0] = None
        phase_barrier()

        ph3b = contextlib.ExitStack()
        cur[0] = ph3b
        w1_s = sbuf("w1_s", (128, NFB, FF), FP16)
        w2_s = sbuf("w2_s", (128, FF // 128, D), FP16)
        t_w12 = sched.add("g", lambda e: [
            e.dma_start(out=w1_s[:], in_=ffn_w1.rearrange("(b p) o -> p b o", p=128)),
            e.dma_start(out=w2_s[:], in_=ffn_w2.rearrange("(b p) o -> p b o", p=128)),
        ], dma=2)
        h2fm_t = sbuf("h2fm", (128, NFB, 512), FP16)
        h2fm = [h2fm_t, h2fm_t]
        _hr = Res("h2fm")
        h2fm_res = [_hr, _hr]
        gelu_sl = sbuf("gelu_sl", (128, FF // 128, 512), FP16)
        gelu_res = Res("gelu")
        xas2_t = sbuf("xas2", (128, 4, D), F32)
        xas2 = [xas2_t, xas2_t]
        _xr = Res("xas2")
        xas2_res = [_xr, _xr]
        ssl = sbuf("ssl", (128, 4, D), F32)
        ssl_res = Res("ssl")

        for tb in range(NTB):
            bi = tb % 2
            t_hl = sched.add("q", (lambda e, tb=tb, bi=bi: [
                e.dma_start(out=h2fm[bi][:, fb, :],
                            in_=P_h2[tb * 512:(tb + 1) * 512,
                                     fb * 128:(fb + 1) * 128], transpose=True)
                for fb in range(NFB)]), deps=h2fm_res[bi].wdeps(), dma=NFB)
            h2fm_res[bi].did_write(t_hl)
            t_xal = sched.add("g", (lambda e, tb=tb, bi=bi: [
                e.dma_start(out=xas2[bi][:],
                            in_=P_xa[tb * 512:(tb + 1) * 512, :]
                            .rearrange("(s p) d -> p s d", p=128))]),
                deps=xas2_res[bi].wdeps(), dma=1)
            xas2_res[bi].did_write(t_xal)

            gl_tasks = []
            for fc in range(FF // 128):
                bk = fc % 2

                def f_mm(e, bi=bi, fc=fc, bk=bk):
                    last = None
                    for dinb in range(NFB):
                        last = e.matmul(
                            bank(bk), w1_s[:, dinb, fc * 128:(fc + 1) * 128],
                            h2fm[bi][:, dinb, :],
                            start=(dinb == 0), stop=(dinb == NFB - 1))
                    return last
                t_mm = sched.add("t", f_mm, deps=[t_hl, t_w12] + bres[bk].wdeps())
                h2fm_res[bi].did_read(t_mm)
                bres[bk].did_write(t_mm)

                def f_gel(e, fc=fc, bk=bk):
                    return e.activation(gelu_sl[:, fc, :], bank(bk), AF.Gelu)
                t_gel = sched.add("a", f_gel,
                                  deps=[t_mm] + (gelu_res.wdeps() if fc == 0
                                                 else []))
                bres[bk].did_read(t_gel)
                gl_tasks.append(t_gel)
            gelu_res.did_write(gl_tasks[-1])

            for sub in range(4):
                for dh in range(2):
                    bk = 2 + (sub * 2 + dh) % 2

                    def f_mm(e, sub=sub, dh=dh, bk=bk):
                        last = None
                        for fc in range(FF // 128):
                            last = e.matmul(
                                bank(bk),
                                gelu_sl[:, fc, sub * 128:(sub + 1) * 128],
                                w2_s[:, fc, dh * 512:(dh + 1) * 512],
                                start=(fc == 0), stop=(fc == FF // 128 - 1))
                        return last
                    t_mm2 = sched.add("t", f_mm,
                                      deps=gl_tasks + [t_w12] + bres[bk].wdeps())
                    gelu_res.did_read(t_mm2)
                    bres[bk].did_write(t_mm2)

                    def f_ev(e, bi=bi, sub=sub, dh=dh, bk=bk):
                        return e.tensor_tensor(
                            ssl[:, sub, dh * 512:(dh + 1) * 512], bank(bk),
                            xas2[bi][:, sub, dh * 512:(dh + 1) * 512], ALU.add)
                    t_ev2 = sched.add("v", f_ev,
                                      deps=[t_mm2, t_xal] + (ssl_res.wdeps()
                                                             if sub == 0 and dh == 0
                                                             else []))
                    bres[bk].did_read(t_ev2)
                    xas2_res[bi].did_read(t_ev2)
            ssl_res.did_write(t_ev2)
            t_ssp = sched.add("g", (lambda e, tb=tb: [
                e.dma_start(out=P_s[tb * 512:(tb + 1) * 512, :]
                            .rearrange("(s p) d -> p s d", p=128),
                            in_=ssl[:])]), deps=[t_ev2], dma=1)
            ssl_res.did_read(t_ssp)

        ph3b.close()
        cur[0] = None
        phase_barrier()

        ph4 = contextlib.ExitStack()
        cur[0] = ph4
        SBT = 2048
        NSB = T // SBT
        xnfm = sbuf("xnfm", (128, NFB, SBT), FP16)
        xnfm_res = Res("xnfm")
        gwt = sbuf("gwt", (128, 2, NFB, HID), FP16)
        gwt_res = Res("gwt")
        lw_s = sbuf("lw_s", (128, HID // 128, D), FP16)
        lw_res = Res("lw")
        act_sl = sbuf("act_sl", (128, HID // 128, 512), FP16)
        act_res = Res("act")
        acc_sl = sbuf("acc_sl", (128, SBT // 128, D), FP16)
        acc_res = Res("acc")
        cc_sl = sbuf("cc_sl", (128, 4, E), F32)
        cc_res = Res("cc_sl")
        evt = sbuf("evt", (128, 512), F32)
        evt_res = Res("evt")

        for sb in range(NSB):
            t_xnl = sched.add("q", (lambda e, sb=sb: [
                e.dma_start(out=xnfm[:, fb, :],
                            in_=P_xn[sb * SBT:(sb + 1) * SBT,
                                     fb * 128:(fb + 1) * 128], transpose=True)
                for fb in range(NFB)]), deps=xnfm_res.wdeps(), dma=NFB)
            xnfm_res.did_write(t_xnl)

            t_acc0 = sched.add("v", lambda e: e.memset(acc_sl[:], 0.0),
                               deps=acc_res.wdeps(), drain=True)
            acc_res.did_write(t_acc0)
            t_lastmac = t_acc0

            for e_ in range(E):
                t_gwl = sched.add("g", (lambda e, e_=e_: [
                    e.dma_start(out=gwt[:, g_, :, :],
                                in_=P_gwT[e_, g_].rearrange(
                                    "(b p) h -> p b h", p=128))
                    for g_ in range(2)]),
                    deps=gwt_res.wdeps(), dma=2)
                gwt_res.did_write(t_gwl)

                t_lwl = sched.add("g", (lambda e, e_=e_: [
                    e.dma_start(out=lw_s[:],
                                in_=linear_w[e_].rearrange(
                                    "(c p) d -> p c d", p=128))]),
                    deps=lw_res.wdeps(), dma=1)
                lw_res.did_write(t_lwl)

                for tq in range(SBT // 512):
                    tq0 = tq * 512
                    t_cl = sched.add("g", (lambda e, sb=sb, tq0=tq0: [
                        e.dma_start(out=cc_sl[:],
                                    in_=P_ct[sb * SBT + tq0:
                                             sb * SBT + tq0 + 512, :]
                                    .rearrange("(s p) e -> p s e", p=128))
                    ]), deps=cc_res.wdeps(), dma=1)
                    cc_res.did_write(t_cl)

                    at_tasks = []
                    act_war = act_res.wdeps()
                    for hc in range(HID // 128):
                        b0 = hc % 2
                        b1 = 2 + hc % 2

                        def f_g01(e, hc=hc, tq0=tq0, b0=b0, b1=b1):
                            last = None
                            for dinb in range(NFB):
                                e.matmul(bank(b0),
                                         gwt[:, 0, dinb, hc * 128:(hc + 1) * 128],
                                         xnfm[:, dinb, tq0:tq0 + 512],
                                         start=(dinb == 0),
                                         stop=(dinb == NFB - 1))
                            for dinb in range(NFB):
                                last = e.matmul(
                                    bank(b1),
                                    gwt[:, 1, dinb, hc * 128:(hc + 1) * 128],
                                    xnfm[:, dinb, tq0:tq0 + 512],
                                    start=(dinb == 0), stop=(dinb == NFB - 1))
                            return last
                        t_g01 = sched.add("t", f_g01,
                                          deps=[t_gwl, t_xnl]
                                          + bres[b0].wdeps() + bres[b1].wdeps())
                        gwt_res.did_read(t_g01)
                        xnfm_res.did_read(t_g01)
                        bres[b0].did_write(t_g01)
                        bres[b1].did_write(t_g01)

                        def f_act1(e, b0=b0):
                            return e.activation(evt[:], bank(b0),
                                                AF.Gelu_apprx_tanh)
                        t_a1 = sched.add("a", f_act1,
                                         deps=[t_g01] + evt_res.wdeps())
                        bres[b0].did_read(t_a1)
                        evt_res.did_write(t_a1)

                        def f_act2(e, hc=hc, b1=b1):
                            return e.tensor_tensor(act_sl[:, hc, :], evt[:],
                                                   bank(b1), ALU.mult)
                        t_a2 = sched.add("v", f_act2,
                                         deps=[t_a1] + act_war)
                        evt_res.did_read(t_a2)
                        bres[b1].did_read(t_a2)
                        at_tasks.append(t_a2)
                    act_res.did_write(at_tasks[-1])
                    if dbg_raw_o and sb == 0 and e_ == 0 and tq == 0:
                        t_adbg = sched.add("g", (lambda e: [
                            e.dma_start(out=P_adbg.rearrange(
                                "(c p) t -> p c t", p=128), in_=act_sl[:])]),
                            deps=[at_tasks[-1]], dma=1)
                        act_res.did_read(t_adbg)

                    for dh in range(2):
                        for su in range(4):
                            bk = 4 + su % 2

                            def f_dn(e, su=su, dh=dh, bk=bk):
                                last = None
                                for hc in range(HID // 128):
                                    last = e.matmul(
                                        bank(bk),
                                        act_sl[:, hc, su * 128:(su + 1) * 128],
                                        lw_s[:, hc, dh * 512:(dh + 1) * 512],
                                        start=(hc == 0),
                                        stop=(hc == HID // 128 - 1))
                                return last
                            t_dn = sched.add("t", f_dn,
                                             deps=at_tasks + [t_lwl]
                                             + bres[bk].wdeps())
                            act_res.did_read(t_dn)
                            lw_res.did_read(t_dn)
                            bres[bk].did_write(t_dn)

                            def f_mac(e, su=su, dh=dh, bk=bk, tq0=tq0,
                                      e_=e_):
                                a = acc_sl[:, tq0 // 128 + su,
                                           dh * 512:(dh + 1) * 512]
                                return e.scalar_tensor_tensor(
                                    a, bank(bk), cc_sl[:, su, e_:e_ + 1], a,
                                    ALU.mult, ALU.add)
                            t_mac = sched.add("v", f_mac,
                                              deps=[t_dn, t_cl, t_lastmac],
                                              drain=True)
                            bres[bk].did_read(t_mac)
                            cc_res.did_read(t_mac)
                            t_lastmac = t_mac
            acc_res.did_write(t_lastmac)

            t_msp = sched.add("g", (lambda e, sb=sb: [
                e.dma_start(out=P_m[sb * SBT:(sb + 1) * SBT, :]
                            .rearrange("(s p) d -> p s d", p=128),
                            in_=acc_sl[:])]), deps=[t_lastmac], dma=1)
            acc_res.did_read(t_msp)

        ph4.close()
        cur[0] = None
        phase_barrier()

        ph5 = contextlib.ExitStack()
        cur[0] = ph5
        xfin = [sbuf(f"xfin{i}", (128, 4, D), F32) for i in range(2)]
        xfin_res = [Res(f"xfin{i}") for i in range(2)]
        sfin = [sbuf(f"sfin{i}", (128, 4, D), F32) for i in range(2)]
        sfin_res = [Res(f"sfin{i}") for i in range(2)]
        mfin = [sbuf(f"mfin{i}", (128, 4, D), FP16) for i in range(2)]
        mfin_res = [Res(f"mfin{i}") for i in range(2)]
        gfin = [sbuf(f"gfin{i}", (128, 4, D), F32) for i in range(2)]
        gfin_res = [Res(f"gfin{i}") for i in range(2)]

        for tb in range(NTB):
            bi = tb % 2
            t_xf = sched.add("q", (lambda e, tb=tb, bi=bi: [
                e.dma_start(out=xfin[bi][:],
                            in_=x_in[tb * 512:(tb + 1) * 512, :]
                            .rearrange("(s p) d -> p s d", p=128))]),
                deps=xfin_res[bi].wdeps(), dma=1)
            xfin_res[bi].did_write(t_xf)
            t_sf = sched.add("g", (lambda e, tb=tb, bi=bi: [
                e.dma_start(out=sfin[bi][:],
                            in_=P_s[tb * 512:(tb + 1) * 512, :]
                            .rearrange("(s p) d -> p s d", p=128)),
                e.dma_start(out=mfin[bi][:],
                            in_=P_m[tb * 512:(tb + 1) * 512, :]
                            .rearrange("(s p) d -> p s d", p=128))]),
                deps=sfin_res[bi].wdeps() + mfin_res[bi].wdeps(), dma=2)
            sfin_res[bi].did_write(t_sf)
            mfin_res[bi].did_write(t_sf)

            def f_sil(e, bi=bi):
                return e.activation(gfin[bi][:], mfin[bi][:], AF.Silu)
            t_sil = sched.add("a", f_sil,
                              deps=[t_sf] + gfin_res[bi].wdeps())
            mfin_res[bi].did_read(t_sil)
            gfin_res[bi].did_write(t_sil)

            def f_fin(e, bi=bi):
                e.tensor_tensor(gfin[bi][:], gfin[bi][:], sfin[bi][:], ALU.mult)
                e.drain()
                return e.tensor_tensor(gfin[bi][:], gfin[bi][:], xfin[bi][:],
                                       ALU.add)
            t_fin2 = sched.add("v", f_fin, deps=[t_sil, t_xf, t_sf],
                               drain=True)
            sfin_res[bi].did_read(t_fin2)
            xfin_res[bi].did_read(t_fin2)
            gfin_res[bi].did_write(t_fin2)

            t_out = sched.add("g", (lambda e, tb=tb, bi=bi: [
                e.dma_start(out=out_t[tb * 512:(tb + 1) * 512, :]
                            .rearrange("(s p) d -> p s d", p=128),
                            in_=gfin[bi][:])]), deps=[t_fin2], dma=1)
            gfin_res[bi].did_read(t_out)

        ph5.close()
        cur[0] = None

        prev = None
        for n in debug_taps:
            deps = sched.lasts() if prev is None else [prev]
            prev = sched.add("g", (lambda e, p=planes[n], d=taps[n]:
                                   [e.dma_start(out=d, in_=p)]),
                             deps=deps, dma=1)

        block = stack.enter_context(nc.Block())
        sched.emit(nc, block, sems)

    return nc



def _vec_cols(inputs):
    def col(v):
        return np.asarray(v, np.float32).reshape(NFB, 128).T
    order = [inputs["L0_attn_ln1_w"], inputs["L0_attn_mu_r"],
             inputs["L0_attn_mu_k"], inputs["L0_attn_mu_v"],
             inputs["L0_attn_mu_w"], inputs["L0_attn_mu_g"],
             np.asarray(inputs["L0_attn_u"], np.float32).reshape(-1),
             inputs["L0_ffn_router_scale"]]
    return np.concatenate([col(v) for v in order], axis=1)


def _consts(inputs):
    col = np.arange(512)
    t = col % 32
    j = (np.arange(128) % 32)[:, None]
    valid = j < 16
    m_low = (valid & (j < t[None, :])).astype(np.uint8)
    m_hi = (valid & (j + 16 < t[None, :])).astype(np.uint8)
    m_D1 = (valid & (t[None, :] < 16) & (j == t[None, :])).astype(np.uint8)
    m_D2 = (valid & (t[None, :] >= 16) & (j == t[None, :] - 16)).astype(np.uint8)
    gw = np.asarray(inputs["L0_attn_gn_w"], np.float32).reshape(H, DK)
    gb = np.asarray(inputs["L0_attn_gn_b"], np.float32).reshape(H, DK)
    perm = np.array([2 * hp + pa for pa in range(2) for hp in range(8)])
    gw_p = gw[perm].reshape(1, D)
    gb_p = gb[perm].reshape(1, D)
    return {
        "vec_cols": np.ascontiguousarray(_vec_cols(inputs)),
        "m_low": m_low, "m_hi": m_hi, "m_D1": m_D1, "m_D2": m_D2,
        "gnw_rep": np.ascontiguousarray(np.tile(gw_p, (128, 1))),
        "gnb_rep": np.ascontiguousarray(np.tile(gb_p, (128, 1))),
        "ln2w_rep": np.ascontiguousarray(np.tile(
            np.asarray(inputs["L0_attn_ln2_w"], np.float32).reshape(1, D),
            (128, 1))),
        "pes_rep": np.ascontiguousarray(np.tile(
            np.asarray(inputs["L0_ffn_per_expert_scale"], np.float32
                       ).reshape(1, E), (128, 1))),
        "ident": np.eye(128, dtype=np.float32),
    }


def _in_maps(inputs):
    x = np.ascontiguousarray(np.asarray(inputs["L0_attn_x"], np.float32))
    shared = {k: np.ascontiguousarray(np.asarray(inputs["L0_attn_" + k],
                                                 np.float32))
              for k in ("Wr", "Wk", "Wv", "Wg", "Wo", "w0", "wA", "wB",
                        "ffn_w1", "ffn_w2")}
    shared["router_w"] = np.ascontiguousarray(
        np.asarray(inputs["L0_ffn_router_w"], np.float32))
    shared["gating_w"] = np.ascontiguousarray(
        np.asarray(inputs["L0_ffn_gating_w"], np.float32))
    shared["linear_w"] = np.ascontiguousarray(
        np.asarray(inputs["L0_ffn_linear_w"], np.float32))
    shared.update(_consts(inputs))
    return [dict(shared, x=x[b]) for b in range(8)]


def kernel(**inputs):
    nc = build_program()
    in_maps = _in_maps(inputs)
    res = run_bass_kernel_spmd(nc, in_maps, core_ids=list(range(8)))
    return np.stack([np.asarray(r["out"], np.float32) for r in res.results])
